# version 03
# baseline (speedup 1.0000x reference)
import numpy as np
import ml_dtypes
from contextlib import ExitStack
import concourse.bass as bass
import concourse.mybir as mybir

F32 = mybir.dt.float32
BF16 = mybir.dt.bfloat16
I32 = mybir.dt.int32
AF = mybir.ActivationFunctionType
ALU = mybir.AluOpType

D = 4096
NCH = 32
SEQ = 8192
NB = 128
INTER = 11008
NI = 86
OWN = 1024
HALO = 832
TOT = OWN + HALO
FTOT = 2048
GROUPS = [512, 512, 512, 512]
NKV, HD, NQH = 8, 128, 32
VERT_SLOTS = 10
ARENA_BLOCKS = 58
ARENA_TOK = ARENA_BLOCKS * 64
MAGIC = 0x5A17C3D1 - 2**32
ROPE_BASE = 1e6
SM_SCALE = 1.0 / 128

def attn_schedule():
    chunks = []
    for cidx in range(16):
        r0, r1 = 2 * cidx - 16, 2 * cidx - 15
        att = set()
        for rb in (r0, r1):
            vert = (rb + 1) % 8 == 0
            for i in range(16):
                if (rb <= i < rb + 16) or (vert and i >= rb):
                    att.add(i)
        if not att:
            att = {0}
        tri = []
        for half, rb in enumerate((r0, r1)):
            if 0 <= rb < 16:
                tri.append((half, rb))
        chunks.append(dict(kind="local", slot=cidx * 2, qlo=min(att) * 64,
                           qhi=(max(att) + 1) * 64, rel=(r0, r1), tri=tri))
    for v in range(5):
        chunks.append(dict(kind="vert", slot=48 + 2 * v, qlo=0, qhi=1024,
                           rel=None, tri=[]))
    chunks.append(dict(kind="xvert", slot=None, qlo=0, qhi=1024, rel=None, tri=[]))
    return chunks

ATT_CHUNKS = attn_schedule()

def host_core_tokens(core):
    p, r = core // 2, core % 2
    own = np.arange(1024 * core, 1024 * (core + 1))
    win0 = (32 * p - 16 + 8 * r) * 64
    win = np.arange(win0, win0 + 512) if win0 >= 0 else np.full(512, -1)
    vert_abs = [b for b in range(7, max(0, 32 * p - 16), 8)]
    mine = vert_abs[0:5] if r == 0 else vert_abs[5:10]
    vt = []
    for s in range(5):
        if s < len(mine):
            vt.append(np.arange(mine[s] * 64, mine[s] * 64 + 64))
        else:
            vt.append(np.full(64, -1))
    halo = np.concatenate([win] + vt)
    assert halo.shape[0] == HALO
    return own, halo

def host_factor_masks(core):
    p = core // 2
    own0_b = 16 * core
    vert_abs = [b for b in range(7, max(0, 32 * p - 16), 8)]
    fm = np.zeros((22, 128, 16), np.float32)
    r = core % 2
    for ci, ch in enumerate(ATT_CHUNKS):
        for half in range(2):
            if ch["kind"] == "local":
                ab = own0_b + ch["rel"][half]
                if not (0 <= ab < NB):
                    continue
            elif ch["kind"] == "vert":
                vi = (ch["slot"] - 48) + half
                if vi >= len(vert_abs):
                    continue
                ab = vert_abs[vi]
            else:
                if r == 0:
                    continue
                ab = 32 * p - 9 if half == 0 else 32 * p - 1
                if ab < 0:
                    continue
            for i in range(16):
                qab = own0_b + i
                if ab <= qab and ((qab - ab < 16) or ((ab + 1) % 8 == 0)):
                    fm[ci, 64 * half:64 * half + 64, i] = 1.0
    return fm.astype(ml_dtypes.bfloat16)

def host_rope_tables(tokens, scale):
    t = np.where(tokens < 0, 0, tokens).astype(np.float64)
    inv = 1.0 / ROPE_BASE ** (np.arange(0, HD, 2) / HD)
    fr = np.outer(t, inv)
    cos = np.concatenate([np.cos(fr), np.cos(fr)], 1) * scale
    sin = np.sin(fr) * scale
    sinm = np.concatenate([-sin, sin], 1)
    return cos.astype(np.float32), sinm.astype(np.float32)

def host_tri():
    kk, qq = np.meshgrid(np.arange(64), np.arange(64), indexing="ij")
    t1 = (qq >= kk).astype(np.float32)
    return np.concatenate([t1, t1], 0).astype(ml_dtypes.bfloat16)

class Ctr:
    def __init__(self, ctx, nc, name):
        self.sem = ctx.enter_context(nc.semaphore(name))
        self.v = 0
    def inc(self, inst, k=1):
        inst.then_inc(self.sem, k)
        self.v += k
        return inst
    def tok(self):
        return (self.sem, self.v)

class SemState:
    def __init__(self, ctrs, queues=()):
        self.ctrs = ctrs
        self.queues = queues
    def save(self):
        return ([c.v for c in self.ctrs], [q.i for q in self.queues])
    def restore(self, st):
        for c, v in zip(self.ctrs, st[0]):
            c.v = v
        for q, i in zip(self.queues, st[1]):
            q.i = i

def wait(eng, tok):
    if tok is None:
        return
    if isinstance(tok, list):
        for t in tok:
            wait(eng, t)
        return
    if tok[1] > 0:
        eng.wait_ge(tok[0], tok[1])

class Q:
    def __init__(self, eng, ctx, nc, name, nslot=8):
        self.eng = eng
        self.slots = [Ctr(ctx, nc, f"{name}{i}") for i in range(nslot)]
        self.n = nslot
        self.i = 0
    def dma(self, out, in_):
        s = self.slots[self.i % self.n]
        if s.v > 0:
            self.eng.wait_ge(s.sem, s.v)
        s.inc(self.eng.dma_start(out=out, in_=in_), 16)
        self.i += 1
        return s.tok()
    def tok(self):
        return [s.tok() for s in self.slots]
    def wait_all(self, eng=None):
        e = eng or self.eng
        for s in self.slots:
            if s.v > 0:
                e.wait_ge(s.sem, s.v)

class B:
    pass

_CHAINED = {"memset", "memzero", "tensor_copy", "tensor_tensor", "tensor_scalar",
            "tensor_scalar_mul", "tensor_scalar_add", "tensor_scalar_sub",
            "tensor_scalar_max", "tensor_scalar_min", "tensor_add", "tensor_sub",
            "tensor_mul", "tensor_max", "tensor_relu", "tensor_reduce",
            "reduce_max", "reduce_sum", "reciprocal", "scalar_tensor_tensor",
            "tensor_single_scalar", "activation", "copy", "mul", "add",
            "affine_select", "iota", "drain", "tensor_tensor_scan", "square", "sqrt"}

class CE:
    def __init__(self, eng, ctr):
        self._eng, self._ctr = eng, ctr
    def __getattr__(self, name):
        f = getattr(self._eng, name)
        if name not in _CHAINED:
            return f
        def wrapped(*a, **k):
            v = self._ctr.v
            inst = f(*a, **k)
            if v > 0:
                inst._wait_ge(self._ctr.sem, v)
            self._ctr.inc(inst)
            return inst
        return wrapped


def build_program(stage="full", detect_races=True, local_arena=False):
    nc = bass.Bass("TRN2", target_bir_lowering=False, debug=False,
                   detect_race_conditions=detect_races)

    def dram(name, shape, dt, kind="Internal", **kw):
        return nc.dram_tensor(name, shape, dt, kind=kind, **kw)

    xT = dram("xT", [D, FTOT], F32, "ExternalInput").ap()
    WgT = dram("WgT", [D, INTER], F32, "ExternalInput").ap()
    WuT = dram("WuT", [D, INTER], F32, "ExternalInput").ap()
    WdT = dram("WdT", [INTER, D], F32, "ExternalInput").ap()
    WqT = dram("WqT", [D, D], F32, "ExternalInput").ap()
    WkvT = dram("WkvT", [D, 2048], F32, "ExternalInput").ap()
    WdnT = dram("WdnT", [D, D], F32, "ExternalInput").ap()
    agt_h = dram("agt_h", [128, NI], F32, "ExternalInput").ap()
    aut_h = dram("aut_h", [128, NI], F32, "ExternalInput").ap()
    ad_h = dram("ad_h", [128, NCH], F32, "ExternalInput").ap()
    bq_h = dram("bq_h", [1, D], F32, "ExternalInput").ap()
    bkv_h = dram("bkv_h", [1, 2048], F32, "ExternalInput").ap()
    bdn_h = dram("bdn_h", [128, NCH], F32, "ExternalInput").ap()
    cosq_h = dram("cosq_h", [OWN, HD], F32, "ExternalInput").ap()
    sinmq_h = dram("sinmq_h", [OWN, HD], F32, "ExternalInput").ap()
    cosk_h = dram("cosk_h", [1920, HD], F32, "ExternalInput").ap()
    sinmk_h = dram("sinmk_h", [1920, HD], F32, "ExternalInput").ap()
    tri_h = dram("tri_h", [128, 64], BF16, "ExternalInput").ap()
    fmask_h = dram("fmask_h", [22, 128, 16], BF16, "ExternalInput").ap()
    nonce_h = dram("nonce_h", [1, 64], I32, "ExternalInput").ap()

    outT = dram("outT", [D, OWN], F32, "ExternalOutput").ap()
    x1T = dram("x1T", [D, FTOT], F32, "ExternalOutput" if stage == "ffnA" else "Internal").ap()
    x2T = dram("x2T", [D, OWN], F32, "ExternalOutput" if stage == "dense" else "Internal").ap()
    ydbg = dram("ydbg", [D, FTOT], BF16, "ExternalOutput" if stage == "ffnA" else "Internal").ap()
    rdbg = dram("rdbg", [8, 512], F32, "ExternalOutput" if stage == "ffnA" else "Internal").ap()
    hsp = dram("hsp", [NI, 128, FTOT], BF16, "ExternalOutput" if stage == "ffnA" else "Internal").ap()
    WgC = dram("WgC", [D, INTER], BF16).ap()
    WuC = dram("WuC", [D, INTER], BF16).ap()
    WdC = dram("WdC", [INTER, D], BF16).ap()
    ar_space = "Local" if local_arena else "Shared"
    arK = dram("arK", [NKV * 128, ARENA_TOK], BF16, addr_space=ar_space).ap()
    arV = dram("arV", [NKV * ARENA_TOK, HD], BF16, addr_space=ar_space).ap()
    arMax = dram("arMax", [2, 64], F32, addr_space=ar_space).ap()
    arFlag = dram("arFlag", [2, 64], I32, addr_space=ar_space).ap()
    dbg_kind = "ExternalOutput" if stage in ("qkv", "attn") else "Internal"
    dbgq = dram("dbgq", [NQH * HD, OWN], BF16, dbg_kind).ap()
    dbgk = dram("dbgk", [NKV * 128, ARENA_TOK], BF16, dbg_kind).ap()
    dbgv = dram("dbgv", [NKV * ARENA_TOK, HD], BF16, dbg_kind).ap()
    dbgctx = dram("dbgctx", [NQH * HD, OWN], BF16,
                  "ExternalOutput" if stage == "attn" else "Internal").ap()
    dbgnb = dram("dbgnb", [128, 8], F32,
                 "ExternalOutput" if stage == "attn" else "Internal").ap()
    dbgpb = dram("dbgpb", [128, 1024], BF16,
                 "ExternalOutput" if stage == "attn" else "Internal").ap()

    ctx = ExitStack()
    sbuf = lambda n, s, d: ctx.enter_context(nc.sbuf_tensor(n, s, d))

    qS = Q(nc.sync, ctx, nc, "qS", nslot=8)
    qP = Q(nc.scalar, ctx, nc, "qP", nslot=8)
    qG = Q(nc.gpsimd, ctx, nc, "qG", nslot=6)
    eT = Ctr(ctx, nc, "eT"); eS = Ctr(ctx, nc, "eS"); eV = Ctr(ctx, nc, "eV")
    eG = Ctr(ctx, nc, "eG")

    T_ = nc.tensor
    S_ = CE(nc.scalar, eS); V_ = CE(nc.vector, eV); G_ = CE(nc.gpsimd, eG)
    SY = nc.sync

    def barrier():
        for tok in (qS.tok(), qP.tok(), qG.tok(), eS.tok(), eV.tok(), eG.tok()):
            wait(T_, tok)
        eT.inc(T_.nop())
        t = eT.tok()
        for e in (S_, V_, G_, SY):
            wait(e, t)

    ones_col = sbuf("ones_col", [128, 1], BF16)
    ones_row = sbuf("ones_row", [1, 128], F32)
    ident = sbuf("ident", [128, 128], BF16)
    agt = sbuf("agt", [128, NI], F32)
    aut = sbuf("aut", [128, NI], F32)
    adh = sbuf("adh", [128, NCH], F32)
    bdn = sbuf("bdn", [128, NCH], F32)
    qS.dma(agt[:], agt_h); qS.dma(aut[:], aut_h)
    qS.dma(adh[:], ad_h); qS.dma(bdn[:], bdn_h)
    V_.memset(ones_col[:], 1.0)
    V_.memset(ones_row[:], 1.0)
    onesb = sbuf("onesb_tmp", [128, 128], BF16)
    G_.memset(ident[:], 0.0)
    G_.memset(onesb[:], 1.0)
    (G_.affine_select(ident[:], onesb[:], [[-1, 128]], ALU.is_equal, 0.0,
                            base=0, channel_multiplier=1))
    wait(V_, qS.tok())
    (V_.tensor_scalar_mul(adh[:], adh[:], 0.5))
    consts_v = eV.tok()
    consts_g = eG.tok()

    def xview(xd, goff, tg):
        return xd[:, goff:goff + tg].rearrange("(c p) t -> p c t", p=128)

    def emit_ffn(xd, x1d, ntok, groups, first_pass, tag=""):
        goffs = np.cumsum([0] + groups)[:-1].tolist()
        wait(S_, qS.tok())
        with ExitStack() as pc:
            sb = lambda n, s, d: pc.enter_context(nc.sbuf_tensor(tag + n, s, d))
            pb = [pc.enter_context(nc.psum_tensor(f"{tag}pb{i}", [128, 512], F32))
                  for i in range(8)]
            ztile = sb("ztile", [128, 512], BF16)
            zrow = sb("zrow", [1, 512], F32)
            V_.memset(ztile[:], 0.0)
            (V_.memset(zrow[:], 0.0))
            z_tok = eV.tok()
            rowA = sb("rowA", [1, 512], F32)
            rowB = sb("rowB", [1, 512], F32)
            tmpn = sb("tmpn", [128, 512], F32)
            pc2 = pc.enter_context(ExitStack())
            sb2 = lambda n, s, d: pc2.enter_context(nc.sbuf_tensor(tag + n, s, d))
            y = sb2("y", [128, NCH * ntok], BF16)
            xsq = [sb2(f"xsq{i}", [128, 512], BF16) for i in range(2)]
            xbc = [sb2(f"xbc{i}", [128, 512], BF16) for i in range(2)]
            xf = [sb2(f"xf{i}", [128, 512], F32) for i in range(2)]
            xc = [sb2(f"xc{i}", [128, 512], F32) for i in range(2)]

            xc_war = [None, None]
            prev_rows_read = None
            for g, (goff, tg) in enumerate(zip(goffs, groups)):
                sq_war = {}
                prev_bcast_read = eV.tok()
                mmi = None
                for k in range(NCH):
                    pa = k % 2
                    if k >= 2:
                        wait(qS.eng, sq_war[k - 2])
                    xtok = qS.dma(xf[pa][:, 0:tg], xd[k*128:(k+1)*128, goff:goff+tg])
                    if k >= 2:
                        wait(V_, sq_war[k - 2])
                    wait(V_, xtok)
                    (V_.tensor_copy(xbc[pa][:, 0:tg], xf[pa][:, 0:tg]))
                    if k >= 2:
                        wait(S_, sq_war[k - 2])
                    wait(S_, eV.tok())
                    if k == 0 and prev_rows_read is not None:
                        wait(T_, prev_rows_read)
                    S_.activation(xsq[pa][:, 0:tg], xbc[pa][:, 0:tg], AF.Square)
                    (S_.drain())
                    wait(T_, eS.tok())
                    if g == 0 and k == 0:
                        wait(T_, consts_v)
                        wait(T_, consts_g)
                    T_.matmul(pb[0][0:1, 0:tg], ones_col[:], xbc[pa][:, 0:tg],
                              start=(k == 0), stop=False)
                    mmi = T_.matmul(pb[1][0:1, 0:tg], ones_col[:], xsq[pa][:, 0:tg],
                                    start=(k == 0), stop=False)
                    eT.inc(mmi)
                    sq_war[k] = eT.tok()
                if g == 0:
                    wait(T_, z_tok)
                T_.matmul(pb[0][0:1, 0:tg], ones_col[:], ztile[:, 0:tg],
                          start=False, stop=True)
                eT.inc(T_.matmul(pb[1][0:1, 0:tg], ones_col[:], ztile[:, 0:tg],
                                 start=False, stop=True))
                stats_tok = eT.tok()
                wait(V_, stats_tok)
                if g == 0:
                    wait(V_, consts_v)
                V_.tensor_scalar_mul(rowA[:, 0:tg], pb[0][0:1, 0:tg], 1.0 / D)
                V_.tensor_scalar_mul(rowB[:, 0:tg], pb[1][0:1, 0:tg], 1.0 / D)
                V_.tensor_tensor(tmpn[0:1, 0:tg], rowA[:, 0:tg], rowA[:, 0:tg], ALU.mult)
                V_.tensor_tensor(rowB[:, 0:tg], rowB[:, 0:tg], tmpn[0:1, 0:tg], ALU.subtract)
                V_.tensor_scalar_add(rowB[:, 0:tg], rowB[:, 0:tg], 1e-5)
                (V_.drain())
                rows_tok = eV.tok()
                prev_rows_read = rows_tok
                wait(S_, rows_tok)
                S_.activation(rowB[:, 0:tg], rowB[:, 0:tg], AF.Ln)
                S_.activation(rowB[:, 0:tg], rowB[:, 0:tg], AF.Exp, scale=-0.5)
                (S_.drain())
                istd_tok = eS.tok()
                if x1d is not outT:
                    wait(SY, istd_tok)
                    qS.dma(rdbg[2*g:2*g+1, 0:tg], rowA[0:1, 0:tg])
                    qS.dma(rdbg[2*g+1:2*g+2, 0:tg], rowB[0:1, 0:tg])
                wait(T_, prev_bcast_read)
                wait(T_, rows_tok)
                T_.matmul(pb[2][:, 0:tg], ones_row[:], rowA[0:1, 0:tg], start=True, stop=False)
                wait(T_, istd_tok)
                T_.matmul(pb[3][:, 0:tg], ones_row[:], rowB[0:1, 0:tg], start=True, stop=False)
                T_.matmul(pb[2][:, 0:tg], ones_row[:], zrow[0:1, 0:tg], start=False, stop=True)
                eT.inc(T_.matmul(pb[3][:, 0:tg], ones_row[:], zrow[0:1, 0:tg], start=False, stop=True))
                bc_tok = eT.tok()
                for k in range(NCH):
                    pa = k % 2
                    if xc_war[pa] is not None:
                        wait(SY, xc_war[pa])
                    xtok = qS.dma(xc[pa][:, 0:tg], xd[k*128:(k+1)*128, goff:goff+tg])
                    wait(V_, xtok)
                    if k == 0:
                        wait(V_, bc_tok)
                    V_.tensor_tensor(tmpn[:, 0:tg], xc[pa][:, 0:tg], pb[2][:, 0:tg], ALU.subtract)
                    (V_.tensor_tensor(y[:, k*ntok+goff:k*ntok+goff+tg], tmpn[:, 0:tg],
                                            pb[3][:, 0:tg], ALU.mult))
                    xc_war[pa] = eV.tok()
            (V_.drain())
            y_tok = eV.tok()
            if x1d is not outT:
                wait(SY, y_tok)
                for k in range(NCH):
                    qS.dma(ydbg[k*128:(k+1)*128, 0:ntok], y[:, k*ntok:(k+1)*ntok])

            wgb = [sb2(f"wgb{i}", [128, NCH * 128], BF16) for i in range(2)]
            wub = [sb2(f"wub{i}", [128, NCH * 128], BF16) for i in range(2)]
            wgf = sb2("wgf", [128, NCH * 128], F32) if first_pass else None
            silu = [sb2(f"silu{i}", [128, 512], F32) for i in range(2)]
            htile = [sb2(f"htile{i}", [128, 512], BF16) for i in range(2)]
            strip_war = [None, None]
            cache_war = [None, None]
            bank_war = [None, None, None, None]
            ht_war = [None, None]
            silu_war = [None, None]
            wq = qG if first_pass else qP
            cast_tok = [None]
            gidx = 0
            for it in range(NI):
                pa = it % 2
                if first_pass:
                    if cast_tok[0] is not None:
                        wait(qP.eng, cast_tok[0][0]); wait(qP.eng, cast_tok[0][1])
                    wg_tok = qP.dma(wgf[:].rearrange("p (c n) -> p c n", n=128), WgT[:, it*128:(it+1)*128].rearrange("(c p) n -> p c n", p=128))
                    if strip_war[pa] is not None:
                        wait(S_, strip_war[pa]); wait(V_, strip_war[pa])
                    if cache_war[pa] is not None:
                        wait(S_, cache_war[pa]); wait(V_, cache_war[pa])
                    wait(S_, wg_tok)
                    S_.copy(wgb[pa][:], wgf[:])
                    (S_.drain())
                    gcast = eS.tok()
                    wait(qP.eng, gcast)
                    wu_tok = qP.dma(wgf[:].rearrange("p (c n) -> p c n", n=128), WuT[:, it*128:(it+1)*128].rearrange("(c p) n -> p c n", p=128))
                    wait(V_, wu_tok)
                    V_.tensor_copy(wub[pa][:], wgf[:])
                    (V_.drain())
                    cast_tok[0] = (gcast, eV.tok())
                    wait(qS.eng, cast_tok[0][0]); wait(qS.eng, cast_tok[0][1])
                    t1 = qS.dma(WgC[:, it*128:(it+1)*128].rearrange("(c p) n -> p c n", p=128), wgb[pa][:].rearrange("p (c n) -> p c n", n=128))
                    t2 = qS.dma(WuC[:, it*128:(it+1)*128].rearrange("(c p) n -> p c n", p=128), wub[pa][:].rearrange("p (c n) -> p c n", n=128))
                    cache_war[pa] = [t1, t2]
                    wait(T_, cast_tok[0][0]); wait(T_, cast_tok[0][1])
                else:
                    if strip_war[pa] is not None:
                        wait(wq.eng, strip_war[pa])
                    t1 = wq.dma(wgb[pa][:].rearrange("p (c n) -> p c n", n=128), WgC[:, it*128:(it+1)*128].rearrange("(c p) n -> p c n", p=128))
                    t2 = wq.dma(wub[pa][:].rearrange("p (c n) -> p c n", n=128), WuC[:, it*128:(it+1)*128].rearrange("(c p) n -> p c n", p=128))
                    w_tok = [t1, t2]
                    wait(T_, w_tok)
                if it == 0:
                    wait(T_, y_tok)
                for g, (goff, tg) in enumerate(zip(goffs, groups)):
                    bi = gidx % 4
                    gidx += 1
                    gb, ub = pb[bi], pb[4 + bi]
                    if bank_war[bi] is not None:
                        wait(T_, bank_war[bi])
                    for k in range(NCH):
                        T_.matmul(gb[:, 0:tg], wgb[pa][:, k*128:(k+1)*128],
                                  y[:, k*ntok+goff:k*ntok+goff+tg],
                                  start=(k == 0), stop=(k == NCH - 1))
                    for k in range(NCH):
                        mmi = T_.matmul(ub[:, 0:tg], wub[pa][:, k*128:(k+1)*128],
                                        y[:, k*ntok+goff:k*ntok+goff+tg],
                                        start=(k == 0), stop=False)
                    eT.inc(T_.matmul(ub[:, 0:tg], ident[:], ztile[:, 0:tg],
                                     start=False, stop=True))
                    gu_tok = eT.tok()
                    wait(S_, gu_tok)
                    if silu_war[pa] is not None:
                        wait(S_, silu_war[pa])
                    S_.activation(silu[pa][:, 0:tg], gb[:, 0:tg], AF.Silu,
                                  scale=agt[:, it:it+1])
                    (S_.drain())
                    wait(V_, eS.tok())
                    if ht_war[pa] is not None:
                        wait(V_, ht_war[pa])
                    V_.scalar_tensor_tensor(htile[pa][:, 0:tg], ub[:, 0:tg],
                                            aut[:, it:it+1], silu[pa][:, 0:tg],
                                            ALU.mult, ALU.mult)
                    (V_.drain())
                    bank_war[bi] = eV.tok()
                    silu_war[pa] = eV.tok()
                    wait(SY, eV.tok())
                    ht_war[pa] = qS.dma(hsp[it, :, goff:goff+tg], htile[pa][:, 0:tg])
                strip_war[pa] = eT.tok()
            pc2.close()

            hg = sb("hg", [128, NI * 512], BF16)
            wdb = [sb(f"wdb{i}", [128, NI * 128], BF16) for i in range(2)]
            wdf = sb("wdf", [128, NI * 128], F32) if first_pass else None
            wd_cast_tok = [None]
            xres = [sb(f"xres{i}", [128, 512], F32) for i in range(2)]
            x1t = [sb(f"x1t{i}", [128, 512], F32) for i in range(2)]
            gateup_done = eT.tok()
            wait(qG.eng, gateup_done)
            wait(qP.eng, gateup_done)
            wait(qP.eng, qS.tok())
            wait(qG.eng, qS.tok())
            wait(qP.eng, qP.tok())
            wq = qG if first_pass else qP
            dn_bank_war = [None, None, None, None]
            xres_war = [None, None]
            x1_st_war = [None, None]
            wd_war = [None, None]
            wd_cache_war = [None, None]
            hg_war = None
            didx = 0
            for g, (goff, tg) in enumerate(zip(goffs, groups)):
                qS.wait_all(SY)
                if hg_war is not None:
                    wait(SY, hg_war)
                hg_tok = qS.dma(hg[:, 0:NI * tg].rearrange("p (i t) -> p i t", t=tg), hsp[:, :, goff:goff+tg].rearrange("i p t -> p i t"))
                for ot in range(NCH):
                    pa = ot % 2
                    fp = first_pass and g == 0
                    if fp:
                        if wd_cast_tok[0] is not None:
                            wait(qP.eng, wd_cast_tok[0])
                        wd_tok = qP.dma(wdf[:].rearrange("p (i n) -> p i n", n=128), WdT[:, ot*128:(ot+1)*128].rearrange("(i p) n -> p i n", p=128))
                        if wd_war[pa] is not None:
                            wait(V_, wd_war[pa])
                        if wd_cache_war[pa] is not None:
                            wait(V_, wd_cache_war[pa])
                        wait(V_, wd_tok)
                        V_.tensor_copy(wdb[pa][:], wdf[:])
                        (V_.drain())
                        wd_cast_tok[0] = eV.tok()
                        wait(qS.eng, wd_cast_tok[0])
                        wd_cache_war[pa] = qS.dma(WdC[:, ot*128:(ot+1)*128].rearrange("(i p) n -> p i n", p=128), wdb[pa][:].rearrange("p (i n) -> p i n", n=128))
                        wd_tok = wd_cast_tok[0]
                    else:
                        wq2 = qP
                        if wd_war[pa] is not None:
                            wait(wq2.eng, wd_war[pa])
                        if wd_cache_war[pa] is not None:
                            wait(wq2.eng, wd_cache_war[pa])
                        wd_tok = wq2.dma(wdb[pa][:].rearrange("p (i n) -> p i n", n=128), WdC[:, ot*128:(ot+1)*128].rearrange("(i p) n -> p i n", p=128))
                    dbi = didx % 4
                    didx += 1
                    if xres_war[pa] is not None:
                        wait(SY, xres_war[pa])
                    xr_tok = qS.dma(xres[pa][:, 0:tg], xd[ot*128:(ot+1)*128, goff:goff+tg])
                    wait(T_, wd_tok); wait(T_, hg_tok)
                    if dn_bank_war[dbi] is not None:
                        wait(T_, dn_bank_war[dbi])
                    bank = pb[dbi]
                    for i in range(NI):
                        mmi = T_.matmul(bank[:, 0:tg], wdb[pa][:, i*128:(i+1)*128],
                                        hg[:, i*tg:(i+1)*tg],
                                        start=(i == 0), stop=False)
                    eT.inc(T_.matmul(bank[:, 0:tg], ident[:], ztile[:, 0:tg],
                                     start=False, stop=True))
                    wait(V_, eT.tok()); wait(V_, xr_tok)
                    if x1_st_war[pa] is not None:
                        wait(V_, x1_st_war[pa])
                    V_.scalar_tensor_tensor(x1t[pa][:, 0:tg], bank[:, 0:tg],
                                            adh[:, ot:ot+1], xres[pa][:, 0:tg],
                                            ALU.mult, ALU.add)
                    (V_.drain())
                    dn_bank_war[dbi] = eV.tok()
                    xres_war[pa] = eV.tok()
                    wait(SY, eV.tok())
                    x1_st_war[pa] = qS.dma(x1d[ot*128:(ot+1)*128, goff:goff+tg], x1t[pa][:, 0:tg])
                    wd_war[pa] = eT.tok()
                hg_war = eT.tok()
            qS.wait_all(SY)

    def emit_attn(stage):
        ac = ExitStack()
        with ac:
            sb = lambda n, s, d: ac.enter_context(nc.sbuf_tensor(n, s, d))
            az = sb("az", [128, 512], BF16)
            tri = sb("tri", [128, 64], BF16)
            fmk = sb("fmk", [128, 22 * 16], BF16)
            nonce_s = sb("nonce_s", [1, 64], I32)
            flag_rd = sb("flag_rd", [1, 64], I32)
            qbuf = sb("qbuf", [128, NQH * 1024], BF16)
            kqmax = sb("kqmax", [128, 2], F32)
            maxrow = sb("maxrow", [2, 128], F32)
            mx2 = sb("mx2", [2, 1], F32)
            orow = sb("orow", [1, 2], F32)
            prow = sb("prow", [1, 2], F32)
            bval = sb("bval", [1, 1], F32)
            negB = sb("negB", [128, 1], F32)
            bounce = nc.dram_tensor("bounce", [128, 2], F32).ap()

            cl = [qS.dma(tri[:], tri_h),
                  qS.dma(fmk[:].rearrange("p (c q) -> p c q", q=16), fmask_h.rearrange("c p q -> p c q")),
                  qS.dma(nonce_s[:], nonce_h)]
            V_.memset(az[:], 0.0)
            V_.memset(kqmax[:], 0.0)
            acv_tok = eV.tok()

            pid_reg = nc.partition_id([mybir.EngineType.SP])
            sstate = SemState(qS.slots + qP.slots + qG.slots + [eT, eS, eV, eG], queues=(qS, qP, qG))
            pol_sem = ac.enter_context(nc.semaphore("pol"))
            fin_sem = ac.enter_context(nc.semaphore("fin"))

            with ExitStack() as p2:
                sb2 = lambda n, s, d: p2.enter_context(nc.sbuf_tensor(n, s, d))
                pA = [p2.enter_context(nc.psum_tensor(f"pA{i}", [128, 512], F32)) for i in range(6)]
                pTb = p2.enter_context(nc.psum_tensor("pTb", [128, 1024], BF16))
                bqr = sb2("bqr", [1, D], BF16)
                bkvr = sb2("bkvr", [1, 2048], BF16)
                ones_row_b = sb2("ones_row_b", [1, 128], BF16)
                zrow_b = sb2("zrow_b", [1, 512], BF16)
                cosq = sb2("cosq", [128, 8 * HD], F32)
                sinmq = sb2("sinmq", [128, 8 * HD], F32)
                cosk = sb2("cosk", [128, 15 * HD], F32)
                sinmk = sb2("sinmk", [128, 15 * HD], F32)
                cl.append(qG.dma(bqr[:], bq_h))
                cl.append(qG.dma(bkvr[:], bkv_h))
                cl.append(qS.dma(cosq[:].rearrange("p (a d) -> p a d", d=HD), cosq_h.rearrange("(a p) d -> p a d", p=128)))
                cl.append(qS.dma(sinmq[:].rearrange("p (a d) -> p a d", d=HD), sinmq_h.rearrange("(a p) d -> p a d", p=128)))
                cl.append(qS.dma(cosk[:].rearrange("p (a d) -> p a d", d=HD), cosk_h.rearrange("(a p) d -> p a d", p=128)))
                cl.append(qS.dma(sinmk[:].rearrange("p (a d) -> p a d", d=HD), sinmk_h.rearrange("(a p) d -> p a d", p=128)))
                V_.memset(ones_row_b[:], 1.0)
                V_.memset(zrow_b[:], 0.0)
                acv_tok2 = eV.tok()
                y2 = sb2("y2", [128, NCH * 512], BF16)
                xsq = [sb2(f"axsq{i}", [128, 512], BF16) for i in range(2)]
                xf = [sb2(f"axf{i}", [128, 512], F32) for i in range(2)]
                xbc = [sb2(f"axbc{i}", [128, 512], BF16) for i in range(2)]
                rowA = sb2("arowA", [1, 512], F32)
                rowB = sb2("arowB", [1, 512], F32)
                tmpn = sb2("atmpn", [128, 512], F32)
                ztile = sb2("aztile", [128, 512], BF16)
                zrow = sb2("azrow", [1, 512], F32)
                wsb = [sb2("wsb0", [128, NCH * 256], BF16)]
                ev32 = [sb2(f"ev32{i}", [128, 256], F32) for i in range(2)]
                rt1 = sb2("rt1", [128, 128], F32)
                rt2 = sb2("rt2", [128, 128], F32)
                rbf = [sb2(f"rbf{i}", [128, 128], BF16) for i in range(2)]
                K_st = sb2("K_st", [128, NKV * 512], BF16)
                V_st = sb2("V_st", [128, NKV * 4 * 128], BF16)
                V_.memset(ztile[:], 0.0)
                V_.memset(zrow[:], 0.0)
                z2_tok = eV.tok()

                def ln_groups(base, tgroups, yntok):
                    prev_rows = None
                    sq_war = {}
                    for goff, tg in tgroups:
                        prev_bcast_read = eV.tok()
                        for k in range(NCH):
                            pa = k % 2
                            if sq_war.get(k - 2) is not None:
                                wait(qS.eng, sq_war[k - 2])
                            xtok = qS.dma(xf[pa][:, 0:tg], x1T[k*128:(k+1)*128, base+goff:base+goff+tg])
                            if sq_war.get(k - 2) is not None:
                                wait(V_, sq_war[k - 2])
                            wait(V_, xtok)
                            V_.tensor_copy(xbc[pa][:, 0:tg], xf[pa][:, 0:tg])
                            if sq_war.get(k - 2) is not None:
                                wait(S_, sq_war[k - 2])
                            wait(S_, eV.tok())
                            S_.activation(xsq[pa][:, 0:tg], xbc[pa][:, 0:tg], AF.Square)
                            S_.drain()
                            wait(T_, eS.tok())
                            if prev_rows is not None and k == 0:
                                wait(T_, prev_rows)
                            if k == 0:
                                wait(T_, z2_tok)
                            T_.matmul(pA[0][0:1, 0:tg], ones_col[:], xbc[pa][:, 0:tg],
                                      start=(k == 0), stop=False)
                            eT.inc(T_.matmul(pA[1][0:1, 0:tg], ones_col[:], xsq[pa][:, 0:tg],
                                             start=(k == 0), stop=False))
                            sq_war[k] = eT.tok()
                        T_.matmul(pA[0][0:1, 0:tg], ones_col[:], ztile[:, 0:tg], start=False, stop=True)
                        eT.inc(T_.matmul(pA[1][0:1, 0:tg], ones_col[:], ztile[:, 0:tg], start=False, stop=True))
                        stats_tok = eT.tok()
                        wait(V_, stats_tok)
                        V_.tensor_scalar_mul(rowA[:, 0:tg], pA[0][0:1, 0:tg], 1.0 / D)
                        V_.tensor_scalar_mul(rowB[:, 0:tg], pA[1][0:1, 0:tg], 1.0 / D)
                        V_.tensor_tensor(tmpn[0:1, 0:tg], rowA[:, 0:tg], rowA[:, 0:tg], ALU.mult)
                        V_.tensor_tensor(rowB[:, 0:tg], rowB[:, 0:tg], tmpn[0:1, 0:tg], ALU.subtract)
                        V_.tensor_scalar_add(rowB[:, 0:tg], rowB[:, 0:tg], 1e-5)
                        V_.drain()
                        rows_tok = eV.tok()
                        prev_rows = rows_tok
                        wait(S_, rows_tok)
                        S_.activation(rowB[:, 0:tg], rowB[:, 0:tg], AF.Ln)
                        S_.activation(rowB[:, 0:tg], rowB[:, 0:tg], AF.Exp, scale=-0.5)
                        S_.drain()
                        istd_tok = eS.tok()
                        wait(T_, prev_bcast_read)
                        wait(T_, rows_tok)
                        T_.matmul(pA[2][:, 0:tg], ones_row[:], rowA[0:1, 0:tg], start=True, stop=False)
                        wait(T_, istd_tok)
                        T_.matmul(pA[3][:, 0:tg], ones_row[:], rowB[0:1, 0:tg], start=True, stop=False)
                        T_.matmul(pA[2][:, 0:tg], ones_row[:], zrow[0:1, 0:tg], start=False, stop=True)
                        eT.inc(T_.matmul(pA[3][:, 0:tg], ones_row[:], zrow[0:1, 0:tg], start=False, stop=True))
                        bc_tok = eT.tok()
                        xc_war = {}
                        for k in range(NCH):
                            pa = k % 2
                            wait(qS.eng, xc_war[k - 2] if (k - 2) in xc_war else sq_war[NCH - 2 + (k % 2)])
                            xtok = qS.dma(xf[pa][:, 0:tg], x1T[k*128:(k+1)*128, base+goff:base+goff+tg])
                            wait(V_, xtok)
                            if k == 0:
                                wait(V_, bc_tok)
                            V_.tensor_tensor(tmpn[:, 0:tg], xf[pa][:, 0:tg], pA[2][:, 0:tg], ALU.subtract)
                            V_.tensor_tensor(y2[:, k*yntok + goff: k*yntok + goff + tg],
                                             tmpn[:, 0:tg], pA[3][:, 0:tg], ALU.mult)
                            xc_war[k] = eV.tok()
                        sq_war = {NCH - 2: xc_war[NCH - 2], NCH - 1: xc_war[NCH - 1]}
                    V_.drain()
                    return eV.tok()

                def rope_tile(src32, cosv, sinv, dst_bf, tsz):
                    V_.tensor_tensor(rt1[0:tsz, 0:128], src32, cosv, ALU.mult)
                    V_.tensor_tensor(rt2[0:tsz, 0:64], src32[:, 64:128], sinv[:, 0:64], ALU.mult)
                    V_.tensor_tensor(rt2[0:tsz, 64:128], src32[:, 0:64], sinv[:, 64:128], ALU.mult)
                    V_.tensor_tensor(dst_bf, rt1[0:tsz, 0:128], rt2[0:tsz, 0:128], ALU.add)

                pend = [None]
                tr_slot = [0]
                def flush_pend():
                    if pend[0] is not None:
                        slc, dst = pend[0]
                        wait(V_, eT.tok())
                        V_.tensor_copy(dst, slc)
                        V_.drain()
                        pend[0] = None
                def do_transpose(src_bf, dst_ap, tsz):
                    s = tr_slot[0]
                    tr_slot[0] = (s + 1) % 3
                    eT.inc(T_.transpose(pTb[0:128, s*256: s*256 + tsz], src_bf, ident[0:tsz, 0:tsz]))
                    flush_pend()
                    pend[0] = (pTb[0:128, s*256: s*256 + tsz], dst_ap)
                def finish_transposes():
                    if pend[0] is not None:
                        eT.inc(T_.transpose(pTb[0:128, 768:896], az[0:128, 0:128], ident[:]))
                        flush_pend()

                units = [(0, 512, 4, True), (512, 512, 4, True),
                         (1024, 512, 4, False), (1536, 320, 3, False)]
                strip_war = [None]
                ps_war = [None, None]
                ev_war = [None, None]
                for half, (hbase, htok, ntt, is_own) in enumerate(units):
                    y_tok = ln_groups(hbase, [(0, htok)], htok)
                    yntok = htok
                    strips = ([("q", s) for s in range(16)] if is_own else []) + \
                             [("kv", s) for s in range(NKV)]
                    for si, (skind, s) in enumerate(strips):
                        pa = 0
                        if strip_war[pa] is not None:
                            wait(qG.eng, strip_war[pa])
                        wsrc = WqT[:, s*256:(s+1)*256] if skind == "q" else WkvT[:, s*256:(s+1)*256]
                        w_tok = qG.dma(wsb[pa][:].rearrange("p (c n) -> p c n", n=256),
                                       wsrc.rearrange("(c p) n -> p c n", p=128))
                        wait(T_, w_tok)
                        if si == 0:
                            wait(T_, y_tok)
                            wait(T_, acv_tok)
                            wait(T_, acv_tok2)
                            if half == 0:
                                for t in cl:
                                    wait(T_, t)
                        brow_src = bqr[0:1, s*256:(s+1)*256] if skind == "q" else bkvr[0:1, s*256:(s+1)*256]
                        for tt in range(ntt):
                            tsz = 128 if (half < 3 or tt < 2) else 64
                            epa = tt % 2
                            pq = pA[4 + epa]
                            if ps_war[epa] is not None:
                                wait(T_, ps_war[epa])
                            for k in range(NCH):
                                T_.matmul(pq[0:tsz, 0:256], y2[:, k*yntok + tt*128: k*yntok + tt*128 + tsz],
                                          wsb[pa][:, k*256:(k+1)*256], start=(k == 0), stop=False)
                            T_.matmul(pq[0:tsz, 0:256], ones_row_b[0:1, 0:tsz], brow_src, start=False, stop=False)
                            eT.inc(T_.matmul(pq[0:tsz, 0:256], ones_row_b[0:1, 0:tsz], zrow_b[0:1, 0:256], start=False, stop=True))
                            qkv_tok = eT.tok()
                            wait(S_, qkv_tok)
                            if ev_war[epa] is not None:
                                wait(S_, ev_war[epa])
                            S_.copy(ev32[epa][0:tsz, :], pq[0:tsz, 0:256])
                            S_.drain()
                            ev_tok = eS.tok()
                            ps_war[epa] = ev_tok
                            gt = hbase // 128 + tt
                            wait(V_, ev_tok)
                            if skind == "q":
                                for hh in range(2):
                                    qh = 2 * s + hh
                                    rope_tile(ev32[epa][0:tsz, hh*128:(hh+1)*128],
                                              cosq[0:tsz, gt*HD:(gt+1)*HD], sinmq[0:tsz, gt*HD:(gt+1)*HD],
                                              rbf[hh][0:tsz, 0:128], tsz)
                                    V_.reduce_sum(tmpn[0:tsz, 0:1], rbf[hh][0:tsz, 0:128], axis=mybir.AxisListType.X,
                                                  apply_absolute_value=True)
                                    V_.tensor_tensor(kqmax[0:tsz, 0:1], kqmax[0:tsz, 0:1], tmpn[0:tsz, 0:1], ALU.max)
                                    V_.drain()
                                    wait(T_, eV.tok())
                                    do_transpose(rbf[hh][0:tsz, 0:128],
                                                 qbuf[:, qh*1024 + hbase + tt*128: qh*1024 + hbase + tt*128 + tsz], tsz)
                            else:
                                kv = s
                                rope_tile(ev32[epa][0:tsz, 0:128],
                                          cosk[0:tsz, gt*HD:(gt+1)*HD], sinmk[0:tsz, gt*HD:(gt+1)*HD],
                                          rbf[0][0:tsz, 0:128], tsz)
                                V_.reduce_max(tmpn[0:tsz, 0:1], rbf[0][0:tsz, 0:128], axis=mybir.AxisListType.X,
                                              apply_absolute_value=True)
                                V_.tensor_tensor(kqmax[0:tsz, 1:2], kqmax[0:tsz, 1:2], tmpn[0:tsz, 0:1], ALU.max)
                                V_.drain()
                                wait(T_, eV.tok())
                                do_transpose(rbf[0][0:tsz, 0:128],
                                             K_st[:, kv*htok + tt*128: kv*htok + tt*128 + tsz], tsz)
                                V_.tensor_copy(V_st[0:tsz, (kv*4 + tt)*128:(kv*4 + tt)*128 + 128],
                                               ev32[epa][0:tsz, 128:256])
                                V_.drain()
                            ev_war[epa] = eV.tok()
                        strip_war[pa] = eT.tok()
                    finish_transposes()
                    kst_done = eV.tok()
                    s0 = sstate.save()
                    end_state = None
                    for arm in nc.sync.Switch(pid_reg, 8):
                        sstate.restore(s0)
                        role = int(arm) % 2
                        wait(SY, kst_done)
                        toks = []
                        roles = [role] if not local_arena else [role, 1 - role]
                        for role2 in roles:
                          if half < 2:
                            abase = 1024 + role2 * 1024 + hbase
                          elif half == 2:
                            abase = role2 * 512
                          else:
                            abase = 3072 + role2 * 320
                          for kv in range(NKV):
                            toks.append(qS.dma(arK[kv*128:(kv+1)*128, abase:abase+htok],
                                               K_st[:, kv*htok:(kv+1)*htok]))
                            nfull = htok // 128
                            if nfull:
                                toks.append(qS.dma(
                                    arV[kv*ARENA_TOK + abase: kv*ARENA_TOK + abase + nfull*128, :].rearrange("(a p) d -> p a d", p=128),
                                    V_st[:, (kv*4)*128:(kv*4+nfull)*128].rearrange("p (a d) -> p a d", d=128)))
                            if htok % 128:
                                toks.append(qS.dma(
                                    arV[kv*ARENA_TOK + abase + nfull*128: kv*ARENA_TOK + abase + htok, :].rearrange("(a p) d -> p a d", p=htok % 128),
                                    V_st[0:htok % 128, (kv*4+nfull)*128:(kv*4+nfull+1)*128].rearrange("p (a d) -> p a d", d=128)))
                        for t in toks:
                            wait(SY, t)
                        if end_state is None:
                            end_state = sstate.save()
                        else:
                            assert sstate.save() == end_state, "switch arms diverged"
                    sstate.restore(end_state)
                    nc.sync.sem_inc(fin_sem, 1)
                    wait(V_, (fin_sem, 1 + half))
                    wait(T_, (fin_sem, 1 + half))
            barrier()

            t = qS.dma(bounce, kqmax[:, 0:2])
            wait(SY, t)
            with nc.allow_non_contiguous_dma(reason="tiny 2x128 transpose read"):
                t = qS.dma(maxrow[0:2, :], bounce.rearrange("p c -> c p"))
            wait(V_, t)
            V_.reduce_max(mx2[0:2, 0:1], maxrow[0:2, :], axis=mybir.AxisListType.X)
            V_.drain()
            mx_tok = eV.tok()
            wait(SY, mx_tok)
            t = qS.dma(bounce[0:1, 0:2], mx2[0:2, 0:1])
            wait(SY, t)
            orow_tok = qS.dma(orow[0:1, 0:2], bounce[0:1, 0:2])
            s0 = sstate.save()
            end_state = None
            for arm in nc.sync.Switch(pid_reg, 8):
                sstate.restore(s0)
                role = int(arm) % 2
                wait(SY, orow_tok)
                for role2 in ([role] if not local_arena else [role, 1 - role]):
                    t1 = qS.dma(arMax[role2:role2+1, 0:2], orow[0:1, 0:2])
                    wait(SY, t1)
                    t2 = qS.dma(arFlag[role2:role2+1, :], nonce_s[:])
                    wait(SY, t2)
                if not local_arena:
                    with (nc.sync.register(f"fv{int(arm)}") as fv,
                          nc.sync.register(f"pc{int(arm)}") as pc,
                          nc.sync.register(f"nn{int(arm)}") as nn,
                          nc.sync.register(f"cond{int(arm)}") as cond):
                        nc.sync.load(nn, nonce_s[0:1, 0:1])
                        nc.sync.reg_mov(pc, 0)
                        nc.sync.reg_mov(cond, 1)
                        with nc.sync.While(cond):
                            nc.sync.dma_start(out=flag_rd[:], in_=arFlag[1-role:2-role, :]).then_inc(pol_sem, 16)
                            nc.sync.reg_add(pc, pc, 16)
                            nc.sync.wait_ge(pol_sem, pc)
                            nc.sync.load(fv, flag_rd[0:1, 0:1])
                            nc.sync.reg_alu(cond, fv, nn, ALU.not_equal)
                        nc.sync.dma_start(out=prow[0:1, 0:2], in_=arMax[1-role:2-role, 0:2]).then_inc(fin_sem, 16)
                        nc.sync.wait_ge(fin_sem, 20)
                else:
                    nc.sync.dma_start(out=prow[0:1, 0:2], in_=arMax[role:role+1, 0:2]).then_inc(fin_sem, 16)
                    nc.sync.wait_ge(fin_sem, 20)
                if end_state is None:
                    end_state = sstate.save()
                else:
                    assert sstate.save() == end_state, "arms diverged"
            sstate.restore(end_state)
            nc.sync.sem_inc(fin_sem, 1)
            wait(V_, (fin_sem, 21))
            V_.tensor_tensor(bval[0:1, 0:1], orow[0:1, 1:2], prow[0:1, 1:2], ALU.max)
            V_.tensor_tensor(bval[0:1, 0:1], bval[0:1, 0:1], orow[0:1, 0:1], ALU.mult)
            V_.tensor_scalar_mul(bval[0:1, 0:1], bval[0:1, 0:1], -1.05)
            V_.drain()
            nb_tok = eV.tok()
            if stage == "qkv":
                for kv in range(NKV):
                    qS.dma(dbgk[kv*128:(kv+1)*128, :], arK[kv*128:(kv+1)*128, :])
                qS.dma(dbgv[:, :], arV[:, :])
                for j in range(NQH):
                    qS.dma(dbgq[j*128:(j+1)*128, 0:1024], qbuf[:, j*1024:(j+1)*1024])
                qS.wait_all(SY)
                barrier()
                return

            order = [8] + list(range(0, 8)) + list(range(9, 16)) + list(range(16, 22))
            with ExitStack() as p3:
                sb3 = lambda n, s, d: p3.enter_context(nc.sbuf_tensor(n, s, d))
                sA = p3.enter_context(nc.psum_tensor("sA", [128, 1024], F32))
                sBk = p3.enter_context(nc.psum_tensor("sBk", [128, 1024], F32))
                pctx = p3.enter_context(nc.psum_tensor("pctx", [128, 1024], F32))
                pden = p3.enter_context(nc.psum_tensor("pden", [1, 1024], F32))
                K_att = sb3("K_att", [128, NKV * 2816], BF16)
                V_att = sb3("V_att", [128, NKV * 22 * 128], BF16)
                Pb = [sb3(f"Pb{i}", [128, 1024], BF16) for i in range(4)]
                sdbg = sb3("sdbg", [128, 1024], F32)
                rb_row = sb3("rb_row", [1, 1024], F32)
                rb_sb = sb3("rb_sb", [128, 1024], F32)
                sbanks = [sA, sBk]

                wait(T_, nb_tok)
                T_.matmul(sA[:, 0:1], ones_row[:], bval[0:1, 0:1], start=True, stop=True)
                eT.inc(T_.matmul(sBk[:, 0:512], ident[:], az[:, 0:512], start=True, stop=True))
                wait(S_, eT.tok())
                S_.copy(negB[:, 0:1], sA[:, 0:1])
                S_.drain()
                negB_tok = eS.tok()
                if stage == "attn":
                    wait(SY, negB_tok)
                    with nc.allow_non_contiguous_dma(reason="debug"):
                        qS.dma(dbgnb[:, 0:1], negB[:, 0:1])
                        qS.dma(dbgnb[0:1, 1:3], orow[0:1, 0:2])
                        qS.dma(dbgnb[0:1, 3:5], prow[0:1, 0:2])
                        qS.dma(dbgnb[0:1, 5:6], bval[0:1, 0:1])

                s0 = sstate.save()
                end_state = None
                for arm in nc.sync.Switch(pid_reg, 8):
                    sstate.restore(s0)
                    role = int(arm) % 2
                    toks = []
                    for kv in range(NKV):
                        lbase = role * 1024
                        toks.append(qS.dma(K_att[:, kv*2816: kv*2816 + 2048],
                                           arK[kv*128:(kv+1)*128, lbase:lbase+2048]))
                        toks.append(qS.dma(K_att[:, kv*2816 + 2048: kv*2816 + 2688],
                                           arK[kv*128:(kv+1)*128, 3072:3712]))
                        toks.append(qS.dma(K_att[:, kv*2816 + 2688: kv*2816 + 2752],
                                           arK[kv*128:(kv+1)*128, 448:512]))
                        toks.append(qS.dma(K_att[:, kv*2816 + 2752: kv*2816 + 2816],
                                           arK[kv*128:(kv+1)*128, 960:1024]))
                        toks.append(qS.dma(
                            V_att[:, (kv*22)*128:(kv*22+16)*128].rearrange("p (a d) -> p a d", d=128),
                            arV[kv*ARENA_TOK + lbase: kv*ARENA_TOK + lbase + 2048, :].rearrange("(a p) d -> p a d", p=128)))
                        toks.append(qS.dma(
                            V_att[:, (kv*22+16)*128:(kv*22+21)*128].rearrange("p (a d) -> p a d", d=128),
                            arV[kv*ARENA_TOK + 3072: kv*ARENA_TOK + 3712, :].rearrange("(a p) d -> p a d", p=128)))
                        toks.append(qS.dma(
                            V_att[0:64, (kv*22+21)*128:(kv*22+22)*128],
                            arV[kv*ARENA_TOK + 448: kv*ARENA_TOK + 512, :]))
                        toks.append(qS.dma(
                            V_att[64:128, (kv*22+21)*128:(kv*22+22)*128],
                            arV[kv*ARENA_TOK + 960: kv*ARENA_TOK + 1024, :]))
                    for t in toks:
                        wait(SY, t)
                    if end_state is None:
                        end_state = sstate.save()
                    else:
                        assert sstate.save() == end_state, "arms diverged"
                sstate.restore(end_state)
                nc.sync.sem_inc(fin_sem, 1)
                wait(T_, (fin_sem, 22))

                def segs(qlo, qhi):
                    out = []
                    if qlo < 512:
                        out.append((qlo, min(qhi, 512)))
                    if qhi > 512:
                        out.append((max(qlo, 512), qhi))
                    return out

                prev_iter_v = [None]
                prev_iter_s = [negB_tok]
                dn_open_war = None
                for kv in range(NKV):
                    for j in range(4):
                        qhg = kv * 4 + j
                        n_ord = len(order)
                        smm_tok = {}
                        exp_tok = {}
                        done_tok = {}

                        if dn_open_war is not None:
                            wait(T_, dn_open_war)
                        T_.matmul(pden[0:1, 0:512], ones_col[:], az[:, 0:512], start=True, stop=False)
                        T_.matmul(pden[0:1, 512:1024], ones_col[:], az[:, 0:512], start=True, stop=False)

                        def emit_smm(oi):
                            ci = order[oi]
                            cpar = oi % 2
                            ch = ATT_CHUNKS[ci]
                            qlo, qhi = ch["qlo"], ch["qhi"]
                            koff = kv*2816 + (ci*128 if ci < 16 else 2048 + (ci-16)*128)
                            if oi >= 2:
                                wait(T_, exp_tok[oi - 2])
                            elif prev_iter_v[0] is not None:
                                wait(T_, prev_iter_v[0])
                                wait(T_, prev_iter_s[0])
                            for s0q, s1q in segs(qlo, qhi):
                                eT.inc(T_.matmul(sbanks[cpar][:, s0q:s1q], K_att[:, koff:koff+128],
                                                 qbuf[:, qhg*1024 + s0q: qhg*1024 + s1q],
                                                 start=True, stop=True))
                            smm_tok[oi] = eT.tok()

                        def proc(oi, lag):
                            ci = order[oi]
                            cpar = oi % 2
                            pbi = oi % 4
                            ch = ATT_CHUNKS[ci]
                            qlo, qhi = ch["qlo"], ch["qhi"]
                            wait(S_, lag)
                            wait(S_, negB_tok)
                            if oi >= 4:
                                wait(S_, done_tok[oi - 4])
                            elif prev_iter_v[0] is not None:
                                wait(S_, prev_iter_v[0])
                            S_.activation(Pb[pbi][:, qlo:qhi], sbanks[cpar][:, qlo:qhi], AF.Exp,
                                          bias=negB[:, 0:1])
                            S_.drain()
                            exp_tok[oi] = eS.tok()
                            wait(V_, exp_tok[oi])
                            pv = Pb[pbi][:, qlo:qhi].rearrange("p (b i) -> p b i", i=64)
                            fv_ = fmk[:, ci*16 + qlo//64: ci*16 + qhi//64].rearrange("p (b one) -> p b one", one=1)
                            hb, fb = bass.broadcast_tensor_aps(pv, fv_)
                            V_.tensor_tensor(pv, hb, fb, ALU.mult)
                            for (hhf, qb) in ch["tri"]:
                                V_.tensor_tensor(Pb[pbi][hhf*64:(hhf+1)*64, qb*64:(qb+1)*64],
                                                 Pb[pbi][hhf*64:(hhf+1)*64, qb*64:(qb+1)*64],
                                                 tri[hhf*64:(hhf+1)*64, 0:64], ALU.mult)
                            V_.drain()
                            mask_tok = eV.tok()
                            wait(T_, mask_tok)
                            voff = (kv*22 + ci)*128
                            first = oi == 0
                            sg = segs(qlo, qhi)
                            for si_, (s0q, s1q) in enumerate(sg):
                                T_.matmul(pctx[:, s0q:s1q], V_att[:, voff:voff+128],
                                          Pb[pbi][:, s0q:s1q], start=first, stop=False)
                                mmd = T_.matmul(pden[0:1, s0q:s1q], ones_col[:], Pb[pbi][:, s0q:s1q],
                                                start=False, stop=False)
                            eT.inc(mmd)
                            done_tok[oi] = eT.tok()

                        for oi in range(n_ord):
                            emit_smm(oi)
                            if oi >= 1:
                                proc(oi - 1, smm_tok[oi])
                        eT.inc(T_.matmul(pden[0:1, 0:512], ones_col[:], az[:, 0:512],
                                         start=False, stop=False))
                        proc(n_ord - 1, eT.tok())

                        for s0q in (0, 512):
                            T_.matmul(pctx[:, s0q:s0q+512], ident[:], az[:, 0:512], start=False,
                                      stop=True)
                            mmz = T_.matmul(pden[0:1, s0q:s0q+512], ones_col[:], az[:, 0:512],
                                            start=False, stop=True)
                        eT.inc(mmz)
                        acc_tok = eT.tok()
                        wait(V_, acc_tok)
                        V_.reciprocal(rb_row[0:1, 0:1024], pden[0:1, 0:1024])
                        V_.drain()
                        rec_tok = eV.tok()
                        wait(T_, rec_tok)
                        T_.matmul(sA[:, 0:512], ones_row[:], rb_row[0:1, 0:512], start=True, stop=True)
                        T_.matmul(sA[:, 512:1024], ones_row[:], rb_row[0:1, 512:1024], start=True, stop=True)
                        eT.inc(T_.matmul(sBk[:, 0:512], ident[:], az[:, 0:512], start=True, stop=True))
                        dn_open_war = None
                        wait(S_, eT.tok())
                        S_.copy(rb_sb[:, 0:1024], sA[:, 0:1024])
                        S_.drain()
                        prev_iter_s[0] = eS.tok()
                        wait(V_, prev_iter_s[0])
                        V_.tensor_tensor(qbuf[:, qhg*1024:(qhg+1)*1024], pctx[:, 0:1024],
                                         rb_sb[:, 0:1024], ALU.mult)
                        V_.drain()
                        prev_iter_v[0] = eV.tok()
                        dn_open_war = prev_iter_v[0]
            barrier()
            if stage == "attn":
                for j in range(NQH):
                    qS.dma(dbgctx[j*128:(j+1)*128, 0:1024], qbuf[:, j*1024:(j+1)*1024])
                qS.wait_all(SY)
                return

            with ExitStack() as p4:
                sb4 = lambda n, s, d: p4.enter_context(nc.sbuf_tensor(n, s, d))
                pD = [p4.enter_context(nc.psum_tensor(f"pD{i}", [128, 512], F32)) for i in range(2)]
                wdn = [sb4(f"wdn{i}", [128, NCH * 128], BF16) for i in range(2)]
                x1r = [sb4(f"x1r{i}", [128, 512], F32) for i in range(2)]
                x2t = [sb4(f"x2t{i}", [128, 512], F32) for i in range(2)]
                azd = sb4("azd", [128, 512], BF16)
                V_.memset(azd[:], 0.0)
                zd_tok = eV.tok()
                dn_war = [None, None]
                st_war = [None, None]
                wv_war = [None, None]
                for ot in range(NCH):
                    pa = ot % 2
                    if wv_war[pa] is not None:
                        wait(qG.eng, wv_war[pa])
                    w_tok = qG.dma(wdn[pa][:].rearrange("p (c n) -> p c n", n=128),
                                   WdnT[:, ot*128:(ot+1)*128].rearrange("(c p) n -> p c n", p=128))
                    wait(T_, w_tok)
                    if ot == 0:
                        wait(T_, zd_tok)
                    for g in range(2):
                        goff = g * 512
                        if dn_war[pa] is not None:
                            wait(SY, dn_war[pa])
                        xr_tok = qS.dma(x1r[pa][:, 0:512], x1T[ot*128:(ot+1)*128, goff:goff+512])
                        if dn_war[pa] is not None:
                            wait(T_, dn_war[pa])
                        for k in range(NCH):
                            T_.matmul(pD[pa][:, 0:512], wdn[pa][:, k*128:(k+1)*128],
                                      qbuf[:, k*1024 + goff: k*1024 + goff + 512],
                                      start=(k == 0), stop=False)
                        eT.inc(T_.matmul(pD[pa][:, 0:512], ident[:], azd[:, 0:512], start=False, stop=True))
                        wait(V_, eT.tok()); wait(V_, xr_tok)
                        if st_war[pa] is not None:
                            wait(V_, st_war[pa])
                        V_.scalar_tensor_tensor(x2t[pa][:, 0:512], pD[pa][:, 0:512], bdn[:, ot:ot+1],
                                                x1r[pa][:, 0:512], ALU.add, ALU.add)
                        V_.drain()
                        dn_war[pa] = eV.tok()
                        wait(SY, eV.tok())
                        st_war[pa] = qS.dma(x2T[ot*128:(ot+1)*128, goff:goff+512], x2t[pa][:, 0:512])
                    wv_war[pa] = eT.tok()
                qS.wait_all(SY)
        return

    emit_ffn(xT, x1T, FTOT, GROUPS, first_pass=True)
    barrier()

    if stage == "ffnA":
        ctx.close()
        return nc

    emit_attn(stage)
    barrier()
    if stage in ("qkv", "attn", "dense"):
        ctx.close()
        return nc

    emit_ffn(x2T, outT, OWN, [512, 512], first_pass=False, tag="fb_")
    barrier()
    ctx.close()
    return nc


def make_in_maps(inp):
    x = np.asarray(inp["L0_attn_hidden_states"])[0]
    w_qkv = np.asarray(inp["L0_attn_w_qkv"])
    b_qkv = np.asarray(inp["L0_attn_b_qkv"])
    w_dn = np.asarray(inp["L0_attn_w_dense"])
    b_dn = np.asarray(inp["L0_attn_b_dense"])
    Wg = np.asarray(inp["L0_ffn_Wg"]); Wu = np.asarray(inp["L0_ffn_Wu"])
    Wd = np.asarray(inp["L0_ffn_Wd"])
    ag = np.asarray(inp["L0_ffn_ag"]); au = np.asarray(inp["L0_ffn_au"])
    ad = np.asarray(inp["L0_ffn_ad"])

    C = np.ascontiguousarray
    WgT = C(Wg.T); WuT = C(Wu.T); WdT = C(Wd.T)
    qrows, bq, kvrows, bkv = [], [], [], []
    for kv in range(NKV):
        for j in range(4):
            qrows.append(w_qkv[kv*768 + j*128: kv*768 + (j+1)*128])
            bq.append(b_qkv[kv*768 + j*128: kv*768 + (j+1)*128])
        kvrows.append(w_qkv[kv*768 + 512: kv*768 + 768])
        bkv.append(b_qkv[kv*768 + 512: kv*768 + 768])
    WqT = C(np.concatenate(qrows, 0).T)
    WkvT = C(np.concatenate(kvrows, 0).T)
    bq = np.concatenate(bq)[None, :]
    bkv = np.concatenate(bkv)[None, :]
    WdnT = C(w_dn.T)

    shared = dict(
        WgT=WgT, WuT=WuT, WdT=WdT, WqT=WqT, WkvT=WkvT, WdnT=WdnT,
        agt_h=C(ag.reshape(NI, 128).T), aut_h=C(au.reshape(NI, 128).T),
        ad_h=C(ad.reshape(NCH, 128).T), bdn_h=C(b_dn.reshape(NCH, 128).T),
        bq_h=C(bq), bkv_h=C(bkv), tri_h=np.asarray(host_tri()),
    )
    nonce = np.full((1, 64), np.random.randint(1, 2**31 - 1), np.int32)
    in_maps = []
    for c in range(8):
        own, halo = host_core_tokens(c)
        allt = np.concatenate([own, halo, np.full(FTOT - TOT, -1)])
        xT = C(gather_tokens(x, allt).T)
        cq, sq = host_rope_tables(own, SM_SCALE)
        kt = np.concatenate([own, halo, np.full(1920 - TOT, -1)])
        ck, sk = host_rope_tables(kt, 1.0)
        m = dict(shared)
        m.update(xT=xT, cosq_h=cq, sinmq_h=sq, cosk_h=ck, sinmk_h=sk,
                 fmask_h=np.asarray(host_factor_masks(c)), nonce_h=nonce)
        in_maps.append(m)
    return in_maps

def assemble_output(results):
    out = np.zeros((SEQ, D), np.float32)
    for c in range(8):
        own, _ = host_core_tokens(c)
        out[own] = np.asarray(results[c]["outT"]).T
    return out[None]

def gather_tokens(x_flat, idx):
    out = np.zeros((len(idx), x_flat.shape[1]), x_flat.dtype)
    ok = idx >= 0
    out[ok] = x_flat[idx[ok]]
    return out


from concourse.bass_utils import run_bass_kernel_spmd
import concourse.bass_utils as _bu

_orig_pnp = _bu._process_ntff_profile
def _pnp_lowmem(*a, **k):
    import gc
    try:
        import jax
        for arr in list(jax.live_arrays()):
            try:
                arr.delete()
            except Exception:
                pass
        jax.clear_caches()
    except Exception:
        pass
    gc.collect()
    try:
        import ctypes
        ctypes.CDLL("libc.so.6").malloc_trim(0)
    except Exception:
        pass
    return _orig_pnp(*a, **k)
_bu._process_ntff_profile = _pnp_lowmem

_NC_CACHE = {}

def kernel(**inputs):
    in_maps = make_in_maps(inputs)
    nc = build_program(stage="full")
    res = run_bass_kernel_spmd(nc, in_maps, core_ids=list(range(8)))
    out = np.zeros((SEQ, D), np.float32)
    for c in range(8):
        own, _ = host_core_tokens(c)
        out[own] = np.asarray(res.results[c]["outT"]).T
    return out[None]


# version 04 (final)
# speedup vs baseline: 1.1669x; 1.1669x over previous
import numpy as np
import ml_dtypes
from contextlib import ExitStack
import concourse.bass as bass
import concourse.mybir as mybir

F32 = mybir.dt.float32
BF16 = mybir.dt.bfloat16
I32 = mybir.dt.int32
AF = mybir.ActivationFunctionType
ALU = mybir.AluOpType

D = 4096
NCH = 32
SEQ = 8192
NB = 128
INTER = 11008
NI = 86
OWN = 1024
HALO = 832
TOT = OWN + HALO
FTOT = 2048
GROUPS = [512, 512, 512, 320]
NKV, HD, NQH = 8, 128, 32
VERT_SLOTS = 10
ARENA_BLOCKS = 58
ARENA_TOK = ARENA_BLOCKS * 64
MAGIC = 0x5A17C3D1 - 2**32
ROPE_BASE = 1e6
SM_SCALE = 1.0 / 128

def attn_schedule():
    chunks = []
    for cidx in range(16):
        r0, r1 = 2 * cidx - 16, 2 * cidx - 15
        att = set()
        for rb in (r0, r1):
            vert = (rb + 1) % 8 == 0
            for i in range(16):
                if (rb <= i < rb + 16) or (vert and i >= rb):
                    att.add(i)
        if not att:
            att = {0}
        tri = []
        for half, rb in enumerate((r0, r1)):
            if 0 <= rb < 16:
                tri.append((half, rb))
        chunks.append(dict(kind="local", slot=cidx * 2, qlo=min(att) * 64,
                           qhi=(max(att) + 1) * 64, rel=(r0, r1), tri=tri))
    for v in range(5):
        chunks.append(dict(kind="vert", slot=48 + 2 * v, qlo=0, qhi=1024,
                           rel=None, tri=[]))
    chunks.append(dict(kind="xvert", slot=None, qlo=0, qhi=1024, rel=None, tri=[]))
    return chunks

ATT_CHUNKS = attn_schedule()

def host_core_tokens(core):
    p, r = core // 2, core % 2
    own = np.arange(1024 * core, 1024 * (core + 1))
    win0 = (32 * p - 16 + 8 * r) * 64
    win = np.arange(win0, win0 + 512) if win0 >= 0 else np.full(512, -1)
    vert_abs = [b for b in range(7, max(0, 32 * p - 16), 8)]
    mine = vert_abs[0:5] if r == 0 else vert_abs[5:10]
    vt = []
    for s in range(5):
        if s < len(mine):
            vt.append(np.arange(mine[s] * 64, mine[s] * 64 + 64))
        else:
            vt.append(np.full(64, -1))
    halo = np.concatenate([win] + vt)
    assert halo.shape[0] == HALO
    return own, halo

def host_factor_masks(core):
    p = core // 2
    own0_b = 16 * core
    vert_abs = [b for b in range(7, max(0, 32 * p - 16), 8)]
    fm = np.zeros((22, 128, 16), np.float32)
    r = core % 2
    for ci, ch in enumerate(ATT_CHUNKS):
        for half in range(2):
            if ch["kind"] == "local":
                ab = own0_b + ch["rel"][half]
                if not (0 <= ab < NB):
                    continue
            elif ch["kind"] == "vert":
                vi = (ch["slot"] - 48) + half
                if vi >= len(vert_abs):
                    continue
                ab = vert_abs[vi]
            else:
                if r == 0:
                    continue
                ab = 32 * p - 9 if half == 0 else 32 * p - 1
                if ab < 0:
                    continue
            for i in range(16):
                qab = own0_b + i
                if ab <= qab and ((qab - ab < 16) or ((ab + 1) % 8 == 0)):
                    fm[ci, 64 * half:64 * half + 64, i] = 1.0
    return fm.astype(ml_dtypes.bfloat16)

def host_rope_tables(tokens, scale):
    t = np.where(tokens < 0, 0, tokens).astype(np.float64)
    inv = 1.0 / ROPE_BASE ** (np.arange(0, HD, 2) / HD)
    fr = np.outer(t, inv)
    cos = np.concatenate([np.cos(fr), np.cos(fr)], 1) * scale
    sin = np.sin(fr) * scale
    sinm = np.concatenate([-sin, sin], 1)
    return cos.astype(np.float32), sinm.astype(np.float32)

def host_tri():
    kk, qq = np.meshgrid(np.arange(64), np.arange(64), indexing="ij")
    t1 = (qq >= kk).astype(np.float32)
    return np.concatenate([t1, t1], 0).astype(ml_dtypes.bfloat16)

class Ctr:
    def __init__(self, ctx, nc, name):
        self.sem = ctx.enter_context(nc.semaphore(name))
        self.v = 0
    def inc(self, inst, k=1):
        inst.then_inc(self.sem, k)
        self.v += k
        return inst
    def tok(self):
        return (self.sem, self.v)

class SemState:
    def __init__(self, ctrs, queues=()):
        self.ctrs = ctrs
        self.queues = queues
    def save(self):
        return ([c.v for c in self.ctrs], [q.i for q in self.queues])
    def restore(self, st):
        for c, v in zip(self.ctrs, st[0]):
            c.v = v
        for q, i in zip(self.queues, st[1]):
            q.i = i

def wait(eng, tok):
    if tok is None:
        return
    if isinstance(tok, list):
        for t in tok:
            wait(eng, t)
        return
    if tok[1] > 0:
        eng.wait_ge(tok[0], tok[1])

class Q:
    def __init__(self, eng, ctx, nc, name, nslot=8):
        self.eng = eng
        self.slots = [Ctr(ctx, nc, f"{name}{i}") for i in range(nslot)]
        self.n = nslot
        self.i = 0
    def dma(self, out, in_):
        s = self.slots[self.i % self.n]
        if s.v > 0:
            self.eng.wait_ge(s.sem, s.v)
        s.inc(self.eng.dma_start(out=out, in_=in_), 16)
        self.i += 1
        return s.tok()
    def tok(self):
        return [s.tok() for s in self.slots]
    def wait_all(self, eng=None):
        e = eng or self.eng
        for s in self.slots:
            if s.v > 0:
                e.wait_ge(s.sem, s.v)

class B:
    pass

_CHAINED = {"memset", "memzero", "tensor_copy", "tensor_tensor", "tensor_scalar",
            "tensor_scalar_mul", "tensor_scalar_add", "tensor_scalar_sub",
            "tensor_scalar_max", "tensor_scalar_min", "tensor_add", "tensor_sub",
            "tensor_mul", "tensor_max", "tensor_relu", "tensor_reduce",
            "reduce_max", "reduce_sum", "reciprocal", "scalar_tensor_tensor",
            "tensor_single_scalar", "activation", "copy", "mul", "add",
            "affine_select", "iota", "drain", "tensor_tensor_scan", "square", "sqrt"}

class CE:
    def __init__(self, eng, ctr):
        self._eng, self._ctr = eng, ctr
    def __getattr__(self, name):
        f = getattr(self._eng, name)
        if name not in _CHAINED:
            return f
        def wrapped(*a, **k):
            v = self._ctr.v
            inst = f(*a, **k)
            if v > 0:
                inst._wait_ge(self._ctr.sem, v)
            self._ctr.inc(inst)
            return inst
        return wrapped


def build_program(stage="full", detect_races=True, local_arena=False):
    nc = bass.Bass("TRN2", target_bir_lowering=False, debug=False,
                   detect_race_conditions=detect_races)

    def dram(name, shape, dt, kind="Internal", **kw):
        return nc.dram_tensor(name, shape, dt, kind=kind, **kw)

    xT = dram("xT", [D, FTOT], F32, "ExternalInput").ap()
    WgT = dram("WgT", [D, INTER], F32, "ExternalInput").ap()
    WuT = dram("WuT", [D, INTER], F32, "ExternalInput").ap()
    WdT = dram("WdT", [INTER, D], F32, "ExternalInput").ap()
    WqT = dram("WqT", [D, D], F32, "ExternalInput").ap()
    WkvT = dram("WkvT", [D, 2048], F32, "ExternalInput").ap()
    WdnT = dram("WdnT", [D, D], F32, "ExternalInput").ap()
    agt_h = dram("agt_h", [128, NI], F32, "ExternalInput").ap()
    aut_h = dram("aut_h", [128, NI], F32, "ExternalInput").ap()
    ad_h = dram("ad_h", [128, NCH], F32, "ExternalInput").ap()
    bq_h = dram("bq_h", [1, D], F32, "ExternalInput").ap()
    bkv_h = dram("bkv_h", [1, 2048], F32, "ExternalInput").ap()
    bdn_h = dram("bdn_h", [128, NCH], F32, "ExternalInput").ap()
    cosq_h = dram("cosq_h", [OWN, HD], F32, "ExternalInput").ap()
    sinmq_h = dram("sinmq_h", [OWN, HD], F32, "ExternalInput").ap()
    cosk_h = dram("cosk_h", [1920, HD], F32, "ExternalInput").ap()
    sinmk_h = dram("sinmk_h", [1920, HD], F32, "ExternalInput").ap()
    tri_h = dram("tri_h", [128, 64], BF16, "ExternalInput").ap()
    fmask_h = dram("fmask_h", [22, 128, 16], BF16, "ExternalInput").ap()
    nonce_h = dram("nonce_h", [1, 64], I32, "ExternalInput").ap()

    outT = dram("outT", [D, OWN], F32, "ExternalOutput").ap()
    x1T = dram("x1T", [D, FTOT], F32, "ExternalOutput" if stage == "ffnA" else "Internal").ap()
    x2T = dram("x2T", [D, OWN], F32, "ExternalOutput" if stage == "dense" else "Internal").ap()
    ydbg = dram("ydbg", [D, FTOT], BF16, "ExternalOutput" if stage == "ffnA" else "Internal").ap()
    rdbg = dram("rdbg", [8, 512], F32, "ExternalOutput" if stage == "ffnA" else "Internal").ap()
    hsp = dram("hsp", [NI, 128, FTOT], BF16, "ExternalOutput" if stage == "ffnA" else "Internal").ap()
    WgC = dram("WgC", [D, INTER], BF16).ap()
    WuC = dram("WuC", [D, INTER], BF16).ap()
    WdC = dram("WdC", [INTER, D], BF16).ap()
    ar_space = "Local" if local_arena else "Shared"
    arK = dram("arK", [NKV * 128, ARENA_TOK], BF16, addr_space=ar_space).ap()
    arV = dram("arV", [NKV * ARENA_TOK, HD], BF16, addr_space=ar_space).ap()
    arMax = dram("arMax", [2, 64], F32, addr_space=ar_space).ap()
    arFlag = dram("arFlag", [2, 64], I32, addr_space=ar_space).ap()
    dbg_kind = "ExternalOutput" if stage in ("qkv", "attn") else "Internal"
    dbgq = dram("dbgq", [NQH * HD, OWN], BF16, dbg_kind).ap()
    dbgk = dram("dbgk", [NKV * 128, ARENA_TOK], BF16, dbg_kind).ap()
    dbgv = dram("dbgv", [NKV * ARENA_TOK, HD], BF16, dbg_kind).ap()
    dbgctx = dram("dbgctx", [NQH * HD, OWN], BF16,
                  "ExternalOutput" if stage == "attn" else "Internal").ap()
    dbgnb = dram("dbgnb", [128, 8], F32,
                 "ExternalOutput" if stage == "attn" else "Internal").ap()
    dbgpb = dram("dbgpb", [128, 1024], BF16,
                 "ExternalOutput" if stage == "attn" else "Internal").ap()

    ctx = ExitStack()
    sbuf = lambda n, s, d: ctx.enter_context(nc.sbuf_tensor(n, s, d))

    qS = Q(nc.sync, ctx, nc, "qS", nslot=8)
    qP = Q(nc.scalar, ctx, nc, "qP", nslot=8)
    qG = Q(nc.gpsimd, ctx, nc, "qG", nslot=6)
    eT = Ctr(ctx, nc, "eT"); eS = Ctr(ctx, nc, "eS"); eV = Ctr(ctx, nc, "eV")
    eG = Ctr(ctx, nc, "eG")

    T_ = nc.tensor
    S_ = CE(nc.scalar, eS); V_ = CE(nc.vector, eV); G_ = CE(nc.gpsimd, eG)
    SY = nc.sync

    def barrier():
        for tok in (qS.tok(), qP.tok(), qG.tok(), eS.tok(), eV.tok(), eG.tok()):
            wait(T_, tok)
        eT.inc(T_.nop())
        t = eT.tok()
        for e in (S_, V_, G_, SY):
            wait(e, t)

    ones_col = sbuf("ones_col", [128, 1], BF16)
    ones_row = sbuf("ones_row", [1, 128], F32)
    ident = sbuf("ident", [128, 128], BF16)
    agt = sbuf("agt", [128, NI], F32)
    aut = sbuf("aut", [128, NI], F32)
    adh = sbuf("adh", [128, NCH], F32)
    bdn = sbuf("bdn", [128, NCH], F32)
    qS.dma(agt[:], agt_h); qS.dma(aut[:], aut_h)
    qS.dma(adh[:], ad_h); qS.dma(bdn[:], bdn_h)
    V_.memset(ones_col[:], 1.0)
    V_.memset(ones_row[:], 1.0)
    onesb = sbuf("onesb_tmp", [128, 128], BF16)
    G_.memset(ident[:], 0.0)
    G_.memset(onesb[:], 1.0)
    (G_.affine_select(ident[:], onesb[:], [[-1, 128]], ALU.is_equal, 0.0,
                            base=0, channel_multiplier=1))
    wait(V_, qS.tok())
    (V_.tensor_scalar_mul(adh[:], adh[:], 0.5))
    consts_v = eV.tok()
    consts_g = eG.tok()

    def xview(xd, goff, tg):
        return xd[:, goff:goff + tg].rearrange("(c p) t -> p c t", p=128)

    def emit_ffn(xd, x1d, ntok, groups, first_pass, tag=""):
        goffs = np.cumsum([0] + groups)[:-1].tolist()
        wait(S_, qS.tok())
        with ExitStack() as pc:
            sb = lambda n, s, d: pc.enter_context(nc.sbuf_tensor(tag + n, s, d))
            pb = [pc.enter_context(nc.psum_tensor(f"{tag}pb{i}", [128, 512], F32))
                  for i in range(8)]
            ztile = sb("ztile", [128, 512], BF16)
            zrow = sb("zrow", [1, 512], F32)
            V_.memset(ztile[:], 0.0)
            (V_.memset(zrow[:], 0.0))
            z_tok = eV.tok()
            rowA = sb("rowA", [1, 512], F32)
            rowB = sb("rowB", [1, 512], F32)
            tmpn = sb("tmpn", [128, 512], F32)
            pc2 = pc.enter_context(ExitStack())
            sb2 = lambda n, s, d: pc2.enter_context(nc.sbuf_tensor(tag + n, s, d))
            y = sb2("y", [128, NCH * ntok], BF16)
            xsq = [sb2(f"xsq{i}", [128, 512], BF16) for i in range(2)]
            xbc = [sb2(f"xbc{i}", [128, 512], BF16) for i in range(2)]
            xf = [sb2(f"xf{i}", [128, 512], F32) for i in range(2)]
            xc = [sb2(f"xc{i}", [128, 512], F32) for i in range(2)]

            xc_war = [None, None]
            prev_rows_read = None
            for g, (goff, tg) in enumerate(zip(goffs, groups)):
                sq_war = {}
                prev_bcast_read = eV.tok()
                mmi = None
                for k in range(NCH):
                    pa = k % 2
                    if k >= 2:
                        wait(qS.eng, sq_war[k - 2])
                    xtok = qS.dma(xf[pa][:, 0:tg], xd[k*128:(k+1)*128, goff:goff+tg])
                    if k >= 2:
                        wait(V_, sq_war[k - 2])
                    wait(V_, xtok)
                    (V_.tensor_copy(xbc[pa][:, 0:tg], xf[pa][:, 0:tg]))
                    if k >= 2:
                        wait(S_, sq_war[k - 2])
                    wait(S_, eV.tok())
                    if k == 0 and prev_rows_read is not None:
                        wait(T_, prev_rows_read)
                    S_.activation(xsq[pa][:, 0:tg], xbc[pa][:, 0:tg], AF.Square)
                    (S_.drain())
                    wait(T_, eS.tok())
                    if g == 0 and k == 0:
                        wait(T_, consts_v)
                        wait(T_, consts_g)
                    T_.matmul(pb[0][0:1, 0:tg], ones_col[:], xbc[pa][:, 0:tg],
                              start=(k == 0), stop=False)
                    mmi = T_.matmul(pb[1][0:1, 0:tg], ones_col[:], xsq[pa][:, 0:tg],
                                    start=(k == 0), stop=False)
                    eT.inc(mmi)
                    sq_war[k] = eT.tok()
                if g == 0:
                    wait(T_, z_tok)
                T_.matmul(pb[0][0:1, 0:tg], ones_col[:], ztile[:, 0:tg],
                          start=False, stop=True)
                eT.inc(T_.matmul(pb[1][0:1, 0:tg], ones_col[:], ztile[:, 0:tg],
                                 start=False, stop=True))
                stats_tok = eT.tok()
                wait(V_, stats_tok)
                if g == 0:
                    wait(V_, consts_v)
                V_.tensor_scalar_mul(rowA[:, 0:tg], pb[0][0:1, 0:tg], 1.0 / D)
                V_.tensor_scalar_mul(rowB[:, 0:tg], pb[1][0:1, 0:tg], 1.0 / D)
                V_.tensor_tensor(tmpn[0:1, 0:tg], rowA[:, 0:tg], rowA[:, 0:tg], ALU.mult)
                V_.tensor_tensor(rowB[:, 0:tg], rowB[:, 0:tg], tmpn[0:1, 0:tg], ALU.subtract)
                V_.tensor_scalar_add(rowB[:, 0:tg], rowB[:, 0:tg], 1e-5)
                (V_.drain())
                rows_tok = eV.tok()
                prev_rows_read = rows_tok
                wait(S_, rows_tok)
                S_.activation(rowB[:, 0:tg], rowB[:, 0:tg], AF.Ln)
                S_.activation(rowB[:, 0:tg], rowB[:, 0:tg], AF.Exp, scale=-0.5)
                (S_.drain())
                istd_tok = eS.tok()
                if stage == "ffnA" and x1d is not outT:
                    wait(SY, istd_tok)
                    qS.dma(rdbg[2*g:2*g+1, 0:tg], rowA[0:1, 0:tg])
                    qS.dma(rdbg[2*g+1:2*g+2, 0:tg], rowB[0:1, 0:tg])
                wait(T_, prev_bcast_read)
                wait(T_, rows_tok)
                T_.matmul(pb[2][:, 0:tg], ones_row[:], rowA[0:1, 0:tg], start=True, stop=False)
                wait(T_, istd_tok)
                T_.matmul(pb[3][:, 0:tg], ones_row[:], rowB[0:1, 0:tg], start=True, stop=False)
                T_.matmul(pb[2][:, 0:tg], ones_row[:], zrow[0:1, 0:tg], start=False, stop=True)
                eT.inc(T_.matmul(pb[3][:, 0:tg], ones_row[:], zrow[0:1, 0:tg], start=False, stop=True))
                bc_tok = eT.tok()
                for k in range(NCH):
                    pa = k % 2
                    if xc_war[pa] is not None:
                        wait(SY, xc_war[pa])
                    xtok = qS.dma(xc[pa][:, 0:tg], xd[k*128:(k+1)*128, goff:goff+tg])
                    wait(V_, xtok)
                    if k == 0:
                        wait(V_, bc_tok)
                    V_.tensor_tensor(tmpn[:, 0:tg], xc[pa][:, 0:tg], pb[2][:, 0:tg], ALU.subtract)
                    (V_.tensor_tensor(y[:, k*ntok+goff:k*ntok+goff+tg], tmpn[:, 0:tg],
                                            pb[3][:, 0:tg], ALU.mult))
                    xc_war[pa] = eV.tok()
            (V_.drain())
            y_tok = eV.tok()
            if stage == "ffnA" and x1d is not outT:
                wait(SY, y_tok)
                tot_y = sum(groups)
                for k in range(NCH):
                    qS.dma(ydbg[k*128:(k+1)*128, 0:tot_y], y[:, k*ntok:k*ntok+tot_y])

            wgb = [sb2(f"wgb{i}", [128, NCH * 128], BF16) for i in range(2)]
            wub = [sb2(f"wub{i}", [128, NCH * 128], BF16) for i in range(2)]
            wgf = sb2("wgf", [128, NCH * 128], F32) if first_pass else None
            silu = [sb2(f"silu{i}", [128, 512], F32) for i in range(2)]
            htile = [sb2(f"htile{i}", [128, 512], BF16) for i in range(2)]
            strip_war = [None, None]
            cache_war = [None, None]
            bank_war = [None, None, None, None]
            ht_war = [None, None]
            silu_war = [None, None]
            wq = qG if first_pass else qP
            cast_tok = [None]
            gidx = 0
            for it in range(NI):
                pa = it % 2
                if first_pass:
                    if cast_tok[0] is not None:
                        wait(qP.eng, cast_tok[0][0]); wait(qP.eng, cast_tok[0][1])
                    wg_tok = qP.dma(wgf[:].rearrange("p (c n) -> p c n", n=128), WgT[:, it*128:(it+1)*128].rearrange("(c p) n -> p c n", p=128))
                    if strip_war[pa] is not None:
                        wait(S_, strip_war[pa]); wait(V_, strip_war[pa])
                    if cache_war[pa] is not None:
                        wait(S_, cache_war[pa]); wait(V_, cache_war[pa])
                    wait(S_, wg_tok)
                    S_.copy(wgb[pa][:], wgf[:])
                    (S_.drain())
                    gcast = eS.tok()
                    wait(qP.eng, gcast)
                    wu_tok = qP.dma(wgf[:].rearrange("p (c n) -> p c n", n=128), WuT[:, it*128:(it+1)*128].rearrange("(c p) n -> p c n", p=128))
                    wait(V_, wu_tok)
                    V_.tensor_copy(wub[pa][:], wgf[:])
                    (V_.drain())
                    cast_tok[0] = (gcast, eV.tok())
                    wait(qS.eng, cast_tok[0][0]); wait(qS.eng, cast_tok[0][1])
                    t1 = qS.dma(WgC[:, it*128:(it+1)*128].rearrange("(c p) n -> p c n", p=128), wgb[pa][:].rearrange("p (c n) -> p c n", n=128))
                    t2 = qS.dma(WuC[:, it*128:(it+1)*128].rearrange("(c p) n -> p c n", p=128), wub[pa][:].rearrange("p (c n) -> p c n", n=128))
                    cache_war[pa] = [t1, t2]
                    wait(T_, cast_tok[0][0]); wait(T_, cast_tok[0][1])
                else:
                    if strip_war[pa] is not None:
                        wait(wq.eng, strip_war[pa])
                    t1 = wq.dma(wgb[pa][:].rearrange("p (c n) -> p c n", n=128), WgC[:, it*128:(it+1)*128].rearrange("(c p) n -> p c n", p=128))
                    t2 = wq.dma(wub[pa][:].rearrange("p (c n) -> p c n", n=128), WuC[:, it*128:(it+1)*128].rearrange("(c p) n -> p c n", p=128))
                    w_tok = [t1, t2]
                    wait(T_, w_tok)
                if it == 0:
                    wait(T_, y_tok)
                for g, (goff, tg) in enumerate(zip(goffs, groups)):
                    bi = gidx % 4
                    gidx += 1
                    gb, ub = pb[bi], pb[4 + bi]
                    if bank_war[bi] is not None:
                        wait(T_, bank_war[bi])
                    for k in range(NCH):
                        T_.matmul(gb[:, 0:tg], wgb[pa][:, k*128:(k+1)*128],
                                  y[:, k*ntok+goff:k*ntok+goff+tg],
                                  start=(k == 0), stop=(k == NCH - 1))
                    for k in range(NCH):
                        mmi = T_.matmul(ub[:, 0:tg], wub[pa][:, k*128:(k+1)*128],
                                        y[:, k*ntok+goff:k*ntok+goff+tg],
                                        start=(k == 0), stop=False)
                    eT.inc(T_.matmul(ub[:, 0:tg], ident[:], ztile[:, 0:tg],
                                     start=False, stop=True))
                    gu_tok = eT.tok()
                    wait(S_, gu_tok)
                    if silu_war[pa] is not None:
                        wait(S_, silu_war[pa])
                    S_.activation(silu[pa][:, 0:tg], gb[:, 0:tg], AF.Silu,
                                  scale=agt[:, it:it+1])
                    (S_.drain())
                    wait(V_, eS.tok())
                    if ht_war[pa] is not None:
                        wait(V_, ht_war[pa])
                    V_.scalar_tensor_tensor(htile[pa][:, 0:tg], ub[:, 0:tg],
                                            aut[:, it:it+1], silu[pa][:, 0:tg],
                                            ALU.mult, ALU.mult)
                    (V_.drain())
                    bank_war[bi] = eV.tok()
                    silu_war[pa] = eV.tok()
                    wait(SY, eV.tok())
                    ht_war[pa] = qS.dma(hsp[it, :, goff:goff+tg], htile[pa][:, 0:tg])
                strip_war[pa] = eT.tok()
            pc2.close()

            hg = sb("hg", [128, NI * 512], BF16)
            wdb = [sb(f"wdb{i}", [128, NI * 128], BF16) for i in range(2)]
            wdf = sb("wdf", [128, NI * 128], F32) if first_pass else None
            wd_cast_tok = [None]
            xres = [sb(f"xres{i}", [128, 512], F32) for i in range(2)]
            x1t = [sb(f"x1t{i}", [128, 512], F32) for i in range(2)]
            gateup_done = eT.tok()
            wait(qG.eng, gateup_done)
            wait(qP.eng, gateup_done)
            wait(qP.eng, qS.tok())
            wait(qG.eng, qS.tok())
            wait(qP.eng, qP.tok())
            wq = qG if first_pass else qP
            dn_bank_war = [None, None, None, None]
            xres_war = [None, None]
            x1_st_war = [None, None]
            wd_war = [None, None]
            wd_cache_war = [None, None]
            hg_war = None
            didx = 0
            for g, (goff, tg) in enumerate(zip(goffs, groups)):
                qS.wait_all(SY)
                if hg_war is not None:
                    wait(SY, hg_war)
                hg_tok = qS.dma(hg[:, 0:NI * tg].rearrange("p (i t) -> p i t", t=tg), hsp[:, :, goff:goff+tg].rearrange("i p t -> p i t"))
                for ot in range(NCH):
                    pa = ot % 2
                    fp = first_pass and g == 0
                    if fp:
                        if wd_cast_tok[0] is not None:
                            wait(qP.eng, wd_cast_tok[0])
                        wd_tok = qP.dma(wdf[:].rearrange("p (i n) -> p i n", n=128), WdT[:, ot*128:(ot+1)*128].rearrange("(i p) n -> p i n", p=128))
                        if wd_war[pa] is not None:
                            wait(V_, wd_war[pa])
                        if wd_cache_war[pa] is not None:
                            wait(V_, wd_cache_war[pa])
                        wait(V_, wd_tok)
                        V_.tensor_copy(wdb[pa][:], wdf[:])
                        (V_.drain())
                        wd_cast_tok[0] = eV.tok()
                        wait(qS.eng, wd_cast_tok[0])
                        wd_cache_war[pa] = qS.dma(WdC[:, ot*128:(ot+1)*128].rearrange("(i p) n -> p i n", p=128), wdb[pa][:].rearrange("p (i n) -> p i n", n=128))
                        wd_tok = wd_cast_tok[0]
                    else:
                        wq2 = qP
                        if wd_war[pa] is not None:
                            wait(wq2.eng, wd_war[pa])
                        if wd_cache_war[pa] is not None:
                            wait(wq2.eng, wd_cache_war[pa])
                        wd_tok = wq2.dma(wdb[pa][:].rearrange("p (i n) -> p i n", n=128), WdC[:, ot*128:(ot+1)*128].rearrange("(i p) n -> p i n", p=128))
                    dbi = didx % 4
                    didx += 1
                    if xres_war[pa] is not None:
                        wait(SY, xres_war[pa])
                    xr_tok = qS.dma(xres[pa][:, 0:tg], xd[ot*128:(ot+1)*128, goff:goff+tg])
                    wait(T_, wd_tok); wait(T_, hg_tok)
                    if dn_bank_war[dbi] is not None:
                        wait(T_, dn_bank_war[dbi])
                    bank = pb[dbi]
                    for i in range(NI):
                        mmi = T_.matmul(bank[:, 0:tg], wdb[pa][:, i*128:(i+1)*128],
                                        hg[:, i*tg:(i+1)*tg],
                                        start=(i == 0), stop=False)
                    eT.inc(T_.matmul(bank[:, 0:tg], ident[:], ztile[:, 0:tg],
                                     start=False, stop=True))
                    wait(V_, eT.tok()); wait(V_, xr_tok)
                    if x1_st_war[pa] is not None:
                        wait(V_, x1_st_war[pa])
                    V_.scalar_tensor_tensor(x1t[pa][:, 0:tg], bank[:, 0:tg],
                                            adh[:, ot:ot+1], xres[pa][:, 0:tg],
                                            ALU.mult, ALU.add)
                    (V_.drain())
                    dn_bank_war[dbi] = eV.tok()
                    xres_war[pa] = eV.tok()
                    wait(SY, eV.tok())
                    x1_st_war[pa] = qS.dma(x1d[ot*128:(ot+1)*128, goff:goff+tg], x1t[pa][:, 0:tg])
                    wd_war[pa] = eT.tok()
                hg_war = eT.tok()
            qS.wait_all(SY)

    def emit_attn(stage):
        ac = ExitStack()
        with ac:
            sb = lambda n, s, d: ac.enter_context(nc.sbuf_tensor(n, s, d))
            az = sb("az", [128, 512], BF16)
            tri = sb("tri", [128, 64], BF16)
            fmk = sb("fmk", [128, 22 * 16], BF16)
            nonce_s = sb("nonce_s", [1, 64], I32)
            flag_rd = sb("flag_rd", [1, 64], I32)
            qbuf = sb("qbuf", [128, NQH * 1024], BF16)
            kqmax = sb("kqmax", [128, 2], F32)
            maxrow = sb("maxrow", [2, 128], F32)
            mx2 = sb("mx2", [2, 1], F32)
            orow = sb("orow", [1, 2], F32)
            prow = sb("prow", [1, 2], F32)
            bval = sb("bval", [1, 1], F32)
            negB = sb("negB", [128, 1], F32)
            bounce = nc.dram_tensor("bounce", [128, 2], F32).ap()

            cl = [qS.dma(tri[:], tri_h),
                  qS.dma(fmk[:].rearrange("p (c q) -> p c q", q=16), fmask_h.rearrange("c p q -> p c q")),
                  qS.dma(nonce_s[:], nonce_h)]
            V_.memset(az[:], 0.0)
            V_.memset(kqmax[:], 0.0)
            acv_tok = eV.tok()

            pid_reg = nc.partition_id([mybir.EngineType.SP])
            sstate = SemState(qS.slots + qP.slots + qG.slots + [eT, eS, eV, eG], queues=(qS, qP, qG))
            pol_sem = ac.enter_context(nc.semaphore("pol"))
            fin_sem = ac.enter_context(nc.semaphore("fin"))

            with ExitStack() as p2:
                sb2 = lambda n, s, d: p2.enter_context(nc.sbuf_tensor(n, s, d))
                pA = [p2.enter_context(nc.psum_tensor(f"pA{i}", [128, 512], F32)) for i in range(6)]
                pTb = p2.enter_context(nc.psum_tensor("pTb", [128, 1024], BF16))
                bqr = sb2("bqr", [1, D], BF16)
                bkvr = sb2("bkvr", [1, 2048], BF16)
                ones_row_b = sb2("ones_row_b", [1, 128], BF16)
                zrow_b = sb2("zrow_b", [1, 512], BF16)
                cosq = sb2("cosq", [128, 8 * HD], F32)
                sinmq = sb2("sinmq", [128, 8 * HD], F32)
                cosk = sb2("cosk", [128, 15 * HD], F32)
                sinmk = sb2("sinmk", [128, 15 * HD], F32)
                cl.append(qG.dma(bqr[:], bq_h))
                cl.append(qG.dma(bkvr[:], bkv_h))
                cl.append(qS.dma(cosq[:].rearrange("p (a d) -> p a d", d=HD), cosq_h.rearrange("(a p) d -> p a d", p=128)))
                cl.append(qS.dma(sinmq[:].rearrange("p (a d) -> p a d", d=HD), sinmq_h.rearrange("(a p) d -> p a d", p=128)))
                cl.append(qS.dma(cosk[:].rearrange("p (a d) -> p a d", d=HD), cosk_h.rearrange("(a p) d -> p a d", p=128)))
                cl.append(qS.dma(sinmk[:].rearrange("p (a d) -> p a d", d=HD), sinmk_h.rearrange("(a p) d -> p a d", p=128)))
                V_.memset(ones_row_b[:], 1.0)
                V_.memset(zrow_b[:], 0.0)
                acv_tok2 = eV.tok()
                y2 = sb2("y2", [128, NCH * 512], BF16)
                xsq = [sb2(f"axsq{i}", [128, 512], BF16) for i in range(2)]
                xf = [sb2(f"axf{i}", [128, 512], F32) for i in range(2)]
                xbc = [sb2(f"axbc{i}", [128, 512], BF16) for i in range(2)]
                rowA = sb2("arowA", [1, 512], F32)
                rowB = sb2("arowB", [1, 512], F32)
                tmpn = sb2("atmpn", [128, 512], F32)
                ztile = sb2("aztile", [128, 512], BF16)
                zrow = sb2("azrow", [1, 512], F32)
                wsb = [sb2("wsb0", [128, NCH * 256], BF16)]
                ev32 = [sb2(f"ev32{i}", [128, 256], F32) for i in range(2)]
                rt1 = sb2("rt1", [128, 128], F32)
                rt2 = sb2("rt2", [128, 128], F32)
                rbf = [sb2(f"rbf{i}", [128, 128], BF16) for i in range(2)]
                K_st = sb2("K_st", [128, NKV * 512], BF16)
                V_st = sb2("V_st", [128, NKV * 4 * 128], BF16)
                V_.memset(ztile[:], 0.0)
                V_.memset(zrow[:], 0.0)
                z2_tok = eV.tok()

                def ln_groups(base, tgroups, yntok):
                    prev_rows = None
                    sq_war = {}
                    for goff, tg in tgroups:
                        prev_bcast_read = eV.tok()
                        for k in range(NCH):
                            pa = k % 2
                            if sq_war.get(k - 2) is not None:
                                wait(qS.eng, sq_war[k - 2])
                            xtok = qS.dma(xf[pa][:, 0:tg], x1T[k*128:(k+1)*128, base+goff:base+goff+tg])
                            if sq_war.get(k - 2) is not None:
                                wait(V_, sq_war[k - 2])
                            wait(V_, xtok)
                            V_.tensor_copy(xbc[pa][:, 0:tg], xf[pa][:, 0:tg])
                            if sq_war.get(k - 2) is not None:
                                wait(S_, sq_war[k - 2])
                            wait(S_, eV.tok())
                            S_.activation(xsq[pa][:, 0:tg], xbc[pa][:, 0:tg], AF.Square)
                            S_.drain()
                            wait(T_, eS.tok())
                            if prev_rows is not None and k == 0:
                                wait(T_, prev_rows)
                            if k == 0:
                                wait(T_, z2_tok)
                            T_.matmul(pA[0][0:1, 0:tg], ones_col[:], xbc[pa][:, 0:tg],
                                      start=(k == 0), stop=False)
                            eT.inc(T_.matmul(pA[1][0:1, 0:tg], ones_col[:], xsq[pa][:, 0:tg],
                                             start=(k == 0), stop=False))
                            sq_war[k] = eT.tok()
                        T_.matmul(pA[0][0:1, 0:tg], ones_col[:], ztile[:, 0:tg], start=False, stop=True)
                        eT.inc(T_.matmul(pA[1][0:1, 0:tg], ones_col[:], ztile[:, 0:tg], start=False, stop=True))
                        stats_tok = eT.tok()
                        wait(V_, stats_tok)
                        V_.tensor_scalar_mul(rowA[:, 0:tg], pA[0][0:1, 0:tg], 1.0 / D)
                        V_.tensor_scalar_mul(rowB[:, 0:tg], pA[1][0:1, 0:tg], 1.0 / D)
                        V_.tensor_tensor(tmpn[0:1, 0:tg], rowA[:, 0:tg], rowA[:, 0:tg], ALU.mult)
                        V_.tensor_tensor(rowB[:, 0:tg], rowB[:, 0:tg], tmpn[0:1, 0:tg], ALU.subtract)
                        V_.tensor_scalar_add(rowB[:, 0:tg], rowB[:, 0:tg], 1e-5)
                        V_.drain()
                        rows_tok = eV.tok()
                        prev_rows = rows_tok
                        wait(S_, rows_tok)
                        S_.activation(rowB[:, 0:tg], rowB[:, 0:tg], AF.Ln)
                        S_.activation(rowB[:, 0:tg], rowB[:, 0:tg], AF.Exp, scale=-0.5)
                        S_.drain()
                        istd_tok = eS.tok()
                        wait(T_, prev_bcast_read)
                        wait(T_, rows_tok)
                        T_.matmul(pA[2][:, 0:tg], ones_row[:], rowA[0:1, 0:tg], start=True, stop=False)
                        wait(T_, istd_tok)
                        T_.matmul(pA[3][:, 0:tg], ones_row[:], rowB[0:1, 0:tg], start=True, stop=False)
                        T_.matmul(pA[2][:, 0:tg], ones_row[:], zrow[0:1, 0:tg], start=False, stop=True)
                        eT.inc(T_.matmul(pA[3][:, 0:tg], ones_row[:], zrow[0:1, 0:tg], start=False, stop=True))
                        bc_tok = eT.tok()
                        xc_war = {}
                        for k in range(NCH):
                            pa = k % 2
                            wait(qS.eng, xc_war[k - 2] if (k - 2) in xc_war else sq_war[NCH - 2 + (k % 2)])
                            xtok = qS.dma(xf[pa][:, 0:tg], x1T[k*128:(k+1)*128, base+goff:base+goff+tg])
                            wait(V_, xtok)
                            if k == 0:
                                wait(V_, bc_tok)
                            V_.tensor_tensor(tmpn[:, 0:tg], xf[pa][:, 0:tg], pA[2][:, 0:tg], ALU.subtract)
                            V_.tensor_tensor(y2[:, k*yntok + goff: k*yntok + goff + tg],
                                             tmpn[:, 0:tg], pA[3][:, 0:tg], ALU.mult)
                            xc_war[k] = eV.tok()
                        sq_war = {NCH - 2: xc_war[NCH - 2], NCH - 1: xc_war[NCH - 1]}
                    V_.drain()
                    return eV.tok()

                def rope_tile(src32, cosv, sinv, dst_bf, tsz):
                    V_.tensor_tensor(rt1[0:tsz, 0:128], src32, cosv, ALU.mult)
                    V_.tensor_tensor(rt2[0:tsz, 0:64], src32[:, 64:128], sinv[:, 0:64], ALU.mult)
                    V_.tensor_tensor(rt2[0:tsz, 64:128], src32[:, 0:64], sinv[:, 64:128], ALU.mult)
                    V_.tensor_tensor(dst_bf, rt1[0:tsz, 0:128], rt2[0:tsz, 0:128], ALU.add)

                pend = [None]
                tr_slot = [0]
                def flush_pend():
                    if pend[0] is not None:
                        slc, dst = pend[0]
                        wait(V_, eT.tok())
                        V_.tensor_copy(dst, slc)
                        V_.drain()
                        pend[0] = None
                def do_transpose(src_bf, dst_ap, tsz):
                    s = tr_slot[0]
                    tr_slot[0] = (s + 1) % 3
                    eT.inc(T_.transpose(pTb[0:128, s*256: s*256 + tsz], src_bf, ident[0:tsz, 0:tsz]))
                    flush_pend()
                    pend[0] = (pTb[0:128, s*256: s*256 + tsz], dst_ap)
                def finish_transposes():
                    if pend[0] is not None:
                        eT.inc(T_.transpose(pTb[0:128, 768:896], az[0:128, 0:128], ident[:]))
                        flush_pend()

                units = [(0, 512, 4, True), (512, 512, 4, True),
                         (1024, 512, 4, False), (1536, 320, 3, False)]
                strip_war = [None]
                ps_war = [None, None]
                ev_war = [None, None]
                for half, (hbase, htok, ntt, is_own) in enumerate(units):
                    y_tok = ln_groups(hbase, [(0, htok)], htok)
                    yntok = htok
                    strips = ([("q", s) for s in range(16)] if is_own else []) + \
                             [("kv", s) for s in range(NKV)]
                    for si, (skind, s) in enumerate(strips):
                        pa = 0
                        if strip_war[pa] is not None:
                            wait(qG.eng, strip_war[pa])
                        wsrc = WqT[:, s*256:(s+1)*256] if skind == "q" else WkvT[:, s*256:(s+1)*256]
                        w_tok = qG.dma(wsb[pa][:].rearrange("p (c n) -> p c n", n=256),
                                       wsrc.rearrange("(c p) n -> p c n", p=128))
                        wait(T_, w_tok)
                        if si == 0:
                            wait(T_, y_tok)
                            wait(T_, acv_tok)
                            wait(T_, acv_tok2)
                            if half == 0:
                                for t in cl:
                                    wait(T_, t)
                        brow_src = bqr[0:1, s*256:(s+1)*256] if skind == "q" else bkvr[0:1, s*256:(s+1)*256]
                        for tt in range(ntt):
                            tsz = 128 if (half < 3 or tt < 2) else 64
                            epa = tt % 2
                            pq = pA[4 + epa]
                            if ps_war[epa] is not None:
                                wait(T_, ps_war[epa])
                            for k in range(NCH):
                                T_.matmul(pq[0:tsz, 0:256], y2[:, k*yntok + tt*128: k*yntok + tt*128 + tsz],
                                          wsb[pa][:, k*256:(k+1)*256], start=(k == 0), stop=False)
                            T_.matmul(pq[0:tsz, 0:256], ones_row_b[0:1, 0:tsz], brow_src, start=False, stop=False)
                            eT.inc(T_.matmul(pq[0:tsz, 0:256], ones_row_b[0:1, 0:tsz], zrow_b[0:1, 0:256], start=False, stop=True))
                            qkv_tok = eT.tok()
                            wait(S_, qkv_tok)
                            if ev_war[epa] is not None:
                                wait(S_, ev_war[epa])
                            S_.copy(ev32[epa][0:tsz, :], pq[0:tsz, 0:256])
                            S_.drain()
                            ev_tok = eS.tok()
                            ps_war[epa] = ev_tok
                            gt = hbase // 128 + tt
                            wait(V_, ev_tok)
                            if skind == "q":
                                for hh in range(2):
                                    qh = 2 * s + hh
                                    rope_tile(ev32[epa][0:tsz, hh*128:(hh+1)*128],
                                              cosq[0:tsz, gt*HD:(gt+1)*HD], sinmq[0:tsz, gt*HD:(gt+1)*HD],
                                              rbf[hh][0:tsz, 0:128], tsz)
                                    V_.reduce_sum(tmpn[0:tsz, 0:1], rbf[hh][0:tsz, 0:128], axis=mybir.AxisListType.X,
                                                  apply_absolute_value=True)
                                    V_.tensor_tensor(kqmax[0:tsz, 0:1], kqmax[0:tsz, 0:1], tmpn[0:tsz, 0:1], ALU.max)
                                    V_.drain()
                                    wait(T_, eV.tok())
                                    do_transpose(rbf[hh][0:tsz, 0:128],
                                                 qbuf[:, qh*1024 + hbase + tt*128: qh*1024 + hbase + tt*128 + tsz], tsz)
                            else:
                                kv = s
                                rope_tile(ev32[epa][0:tsz, 0:128],
                                          cosk[0:tsz, gt*HD:(gt+1)*HD], sinmk[0:tsz, gt*HD:(gt+1)*HD],
                                          rbf[0][0:tsz, 0:128], tsz)
                                V_.reduce_max(tmpn[0:tsz, 0:1], rbf[0][0:tsz, 0:128], axis=mybir.AxisListType.X,
                                              apply_absolute_value=True)
                                V_.tensor_tensor(kqmax[0:tsz, 1:2], kqmax[0:tsz, 1:2], tmpn[0:tsz, 0:1], ALU.max)
                                V_.drain()
                                wait(T_, eV.tok())
                                do_transpose(rbf[0][0:tsz, 0:128],
                                             K_st[:, kv*htok + tt*128: kv*htok + tt*128 + tsz], tsz)
                                V_.tensor_copy(V_st[0:tsz, (kv*4 + tt)*128:(kv*4 + tt)*128 + 128],
                                               ev32[epa][0:tsz, 128:256])
                                V_.drain()
                            ev_war[epa] = eV.tok()
                        strip_war[pa] = eT.tok()
                    finish_transposes()
                    kst_done = eV.tok()
                    s0 = sstate.save()
                    end_state = None
                    for arm in nc.sync.Switch(pid_reg, 8):
                        sstate.restore(s0)
                        role = int(arm) % 2
                        wait(SY, kst_done)
                        toks = []
                        roles = [role] if not local_arena else [role, 1 - role]
                        for role2 in roles:
                          if half < 2:
                            abase = 1024 + role2 * 1024 + hbase
                          elif half == 2:
                            abase = role2 * 512
                          else:
                            abase = 3072 + role2 * 320
                          for kv in range(NKV):
                            toks.append(qS.dma(arK[kv*128:(kv+1)*128, abase:abase+htok],
                                               K_st[:, kv*htok:(kv+1)*htok]))
                            nfull = htok // 128
                            if nfull:
                                toks.append(qS.dma(
                                    arV[kv*ARENA_TOK + abase: kv*ARENA_TOK + abase + nfull*128, :].rearrange("(a p) d -> p a d", p=128),
                                    V_st[:, (kv*4)*128:(kv*4+nfull)*128].rearrange("p (a d) -> p a d", d=128)))
                            if htok % 128:
                                toks.append(qS.dma(
                                    arV[kv*ARENA_TOK + abase + nfull*128: kv*ARENA_TOK + abase + htok, :].rearrange("(a p) d -> p a d", p=htok % 128),
                                    V_st[0:htok % 128, (kv*4+nfull)*128:(kv*4+nfull+1)*128].rearrange("p (a d) -> p a d", d=128)))
                        for t in toks:
                            wait(SY, t)
                        if end_state is None:
                            end_state = sstate.save()
                        else:
                            assert sstate.save() == end_state, "switch arms diverged"
                    sstate.restore(end_state)
                    nc.sync.sem_inc(fin_sem, 1)
                    wait(V_, (fin_sem, 1 + half))
                    wait(T_, (fin_sem, 1 + half))
            barrier()

            t = qS.dma(bounce, kqmax[:, 0:2])
            wait(SY, t)
            with nc.allow_non_contiguous_dma(reason="tiny 2x128 transpose read"):
                t = qS.dma(maxrow[0:2, :], bounce.rearrange("p c -> c p"))
            wait(V_, t)
            V_.reduce_max(mx2[0:2, 0:1], maxrow[0:2, :], axis=mybir.AxisListType.X)
            V_.drain()
            mx_tok = eV.tok()
            wait(SY, mx_tok)
            t = qS.dma(bounce[0:1, 0:2], mx2[0:2, 0:1])
            wait(SY, t)
            orow_tok = qS.dma(orow[0:1, 0:2], bounce[0:1, 0:2])
            s0 = sstate.save()
            end_state = None
            for arm in nc.sync.Switch(pid_reg, 8):
                sstate.restore(s0)
                role = int(arm) % 2
                wait(SY, orow_tok)
                for role2 in ([role] if not local_arena else [role, 1 - role]):
                    t1 = qS.dma(arMax[role2:role2+1, 0:2], orow[0:1, 0:2])
                    wait(SY, t1)
                    t2 = qS.dma(arFlag[role2:role2+1, :], nonce_s[:])
                    wait(SY, t2)
                if not local_arena:
                    with (nc.sync.register(f"fv{int(arm)}") as fv,
                          nc.sync.register(f"pc{int(arm)}") as pc,
                          nc.sync.register(f"nn{int(arm)}") as nn,
                          nc.sync.register(f"cond{int(arm)}") as cond):
                        nc.sync.load(nn, nonce_s[0:1, 0:1])
                        nc.sync.reg_mov(pc, 0)
                        nc.sync.reg_mov(cond, 1)
                        with nc.sync.While(cond):
                            nc.sync.dma_start(out=flag_rd[:], in_=arFlag[1-role:2-role, :]).then_inc(pol_sem, 16)
                            nc.sync.reg_add(pc, pc, 16)
                            nc.sync.wait_ge(pol_sem, pc)
                            nc.sync.load(fv, flag_rd[0:1, 0:1])
                            nc.sync.reg_alu(cond, fv, nn, ALU.not_equal)
                        nc.sync.dma_start(out=prow[0:1, 0:2], in_=arMax[1-role:2-role, 0:2]).then_inc(fin_sem, 16)
                        nc.sync.wait_ge(fin_sem, 20)
                else:
                    nc.sync.dma_start(out=prow[0:1, 0:2], in_=arMax[role:role+1, 0:2]).then_inc(fin_sem, 16)
                    nc.sync.wait_ge(fin_sem, 20)
                if end_state is None:
                    end_state = sstate.save()
                else:
                    assert sstate.save() == end_state, "arms diverged"
            sstate.restore(end_state)
            nc.sync.sem_inc(fin_sem, 1)
            wait(V_, (fin_sem, 21))
            V_.tensor_tensor(bval[0:1, 0:1], orow[0:1, 1:2], prow[0:1, 1:2], ALU.max)
            V_.tensor_tensor(bval[0:1, 0:1], bval[0:1, 0:1], orow[0:1, 0:1], ALU.mult)
            V_.tensor_scalar_mul(bval[0:1, 0:1], bval[0:1, 0:1], -1.05)
            V_.drain()
            nb_tok = eV.tok()
            if stage == "qkv":
                for kv in range(NKV):
                    qS.dma(dbgk[kv*128:(kv+1)*128, :], arK[kv*128:(kv+1)*128, :])
                qS.dma(dbgv[:, :], arV[:, :])
                for j in range(NQH):
                    qS.dma(dbgq[j*128:(j+1)*128, 0:1024], qbuf[:, j*1024:(j+1)*1024])
                qS.wait_all(SY)
                barrier()
                return

            order = [8] + list(range(0, 8)) + list(range(9, 16)) + list(range(16, 22))
            with ExitStack() as p3:
                sb3 = lambda n, s, d: p3.enter_context(nc.sbuf_tensor(n, s, d))
                sA = p3.enter_context(nc.psum_tensor("sA", [128, 1024], F32))
                sBk = p3.enter_context(nc.psum_tensor("sBk", [128, 1024], F32))
                pctx = p3.enter_context(nc.psum_tensor("pctx", [128, 1024], F32))
                pden = p3.enter_context(nc.psum_tensor("pden", [1, 1024], F32))
                K_att = sb3("K_att", [128, NKV * 2816], BF16)
                V_att = sb3("V_att", [128, NKV * 22 * 128], BF16)
                Pb = [sb3(f"Pb{i}", [128, 1024], BF16) for i in range(4)]
                sdbg = sb3("sdbg", [128, 1024], F32)
                rb_row = sb3("rb_row", [1, 1024], F32)
                rb_sb = sb3("rb_sb", [128, 1024], F32)
                sbanks = [sA, sBk]

                wait(T_, nb_tok)
                T_.matmul(sA[:, 0:1], ones_row[:], bval[0:1, 0:1], start=True, stop=True)
                eT.inc(T_.matmul(sBk[:, 0:512], ident[:], az[:, 0:512], start=True, stop=True))
                wait(S_, eT.tok())
                S_.copy(negB[:, 0:1], sA[:, 0:1])
                S_.drain()
                negB_tok = eS.tok()
                if stage == "attn":
                    wait(SY, negB_tok)
                    with nc.allow_non_contiguous_dma(reason="debug"):
                        qS.dma(dbgnb[:, 0:1], negB[:, 0:1])
                        qS.dma(dbgnb[0:1, 1:3], orow[0:1, 0:2])
                        qS.dma(dbgnb[0:1, 3:5], prow[0:1, 0:2])
                        qS.dma(dbgnb[0:1, 5:6], bval[0:1, 0:1])

                s0 = sstate.save()
                end_state = None
                for arm in nc.sync.Switch(pid_reg, 8):
                    sstate.restore(s0)
                    role = int(arm) % 2
                    toks = []
                    for kv in range(NKV):
                        lbase = role * 1024
                        toks.append(qS.dma(K_att[:, kv*2816: kv*2816 + 2048],
                                           arK[kv*128:(kv+1)*128, lbase:lbase+2048]))
                        toks.append(qS.dma(K_att[:, kv*2816 + 2048: kv*2816 + 2688],
                                           arK[kv*128:(kv+1)*128, 3072:3712]))
                        toks.append(qS.dma(K_att[:, kv*2816 + 2688: kv*2816 + 2752],
                                           arK[kv*128:(kv+1)*128, 448:512]))
                        toks.append(qS.dma(K_att[:, kv*2816 + 2752: kv*2816 + 2816],
                                           arK[kv*128:(kv+1)*128, 960:1024]))
                        toks.append(qS.dma(
                            V_att[:, (kv*22)*128:(kv*22+16)*128].rearrange("p (a d) -> p a d", d=128),
                            arV[kv*ARENA_TOK + lbase: kv*ARENA_TOK + lbase + 2048, :].rearrange("(a p) d -> p a d", p=128)))
                        toks.append(qS.dma(
                            V_att[:, (kv*22+16)*128:(kv*22+21)*128].rearrange("p (a d) -> p a d", d=128),
                            arV[kv*ARENA_TOK + 3072: kv*ARENA_TOK + 3712, :].rearrange("(a p) d -> p a d", p=128)))
                        toks.append(qS.dma(
                            V_att[0:64, (kv*22+21)*128:(kv*22+22)*128],
                            arV[kv*ARENA_TOK + 448: kv*ARENA_TOK + 512, :]))
                        toks.append(qS.dma(
                            V_att[64:128, (kv*22+21)*128:(kv*22+22)*128],
                            arV[kv*ARENA_TOK + 960: kv*ARENA_TOK + 1024, :]))
                    for t in toks:
                        wait(SY, t)
                    if end_state is None:
                        end_state = sstate.save()
                    else:
                        assert sstate.save() == end_state, "arms diverged"
                sstate.restore(end_state)
                nc.sync.sem_inc(fin_sem, 1)
                wait(T_, (fin_sem, 22))

                def segs(qlo, qhi):
                    out = []
                    if qlo < 512:
                        out.append((qlo, min(qhi, 512)))
                    if qhi > 512:
                        out.append((max(qlo, 512), qhi))
                    return out

                prev_iter_v = [None]
                prev_iter_s = [negB_tok]
                dn_open_war = None
                for kv in range(NKV):
                    for j in range(4):
                        qhg = kv * 4 + j
                        n_ord = len(order)
                        smm_tok = {}
                        exp_tok = {}
                        done_tok = {}

                        if dn_open_war is not None:
                            wait(T_, dn_open_war)
                        T_.matmul(pden[0:1, 0:512], ones_col[:], az[:, 0:512], start=True, stop=False)
                        T_.matmul(pden[0:1, 512:1024], ones_col[:], az[:, 0:512], start=True, stop=False)

                        def emit_smm(oi):
                            ci = order[oi]
                            cpar = oi % 2
                            ch = ATT_CHUNKS[ci]
                            qlo, qhi = ch["qlo"], ch["qhi"]
                            koff = kv*2816 + (ci*128 if ci < 16 else 2048 + (ci-16)*128)
                            if oi >= 2:
                                wait(T_, exp_tok[oi - 2])
                            elif prev_iter_v[0] is not None:
                                wait(T_, prev_iter_v[0])
                                wait(T_, prev_iter_s[0])
                            for s0q, s1q in segs(qlo, qhi):
                                eT.inc(T_.matmul(sbanks[cpar][:, s0q:s1q], K_att[:, koff:koff+128],
                                                 qbuf[:, qhg*1024 + s0q: qhg*1024 + s1q],
                                                 start=True, stop=True))
                            smm_tok[oi] = eT.tok()

                        def proc(oi, lag):
                            ci = order[oi]
                            cpar = oi % 2
                            pbi = oi % 4
                            ch = ATT_CHUNKS[ci]
                            qlo, qhi = ch["qlo"], ch["qhi"]
                            wait(S_, lag)
                            wait(S_, negB_tok)
                            if oi >= 4:
                                wait(S_, done_tok[oi - 4])
                            elif prev_iter_v[0] is not None:
                                wait(S_, prev_iter_v[0])
                            S_.activation(Pb[pbi][:, qlo:qhi], sbanks[cpar][:, qlo:qhi], AF.Exp,
                                          bias=negB[:, 0:1])
                            S_.drain()
                            exp_tok[oi] = eS.tok()
                            wait(V_, exp_tok[oi])
                            pv = Pb[pbi][:, qlo:qhi].rearrange("p (b i) -> p b i", i=64)
                            fv_ = fmk[:, ci*16 + qlo//64: ci*16 + qhi//64].rearrange("p (b one) -> p b one", one=1)
                            hb, fb = bass.broadcast_tensor_aps(pv, fv_)
                            V_.tensor_tensor(pv, hb, fb, ALU.mult)
                            for (hhf, qb) in ch["tri"]:
                                V_.tensor_tensor(Pb[pbi][hhf*64:(hhf+1)*64, qb*64:(qb+1)*64],
                                                 Pb[pbi][hhf*64:(hhf+1)*64, qb*64:(qb+1)*64],
                                                 tri[hhf*64:(hhf+1)*64, 0:64], ALU.mult)
                            V_.drain()
                            mask_tok = eV.tok()
                            wait(T_, mask_tok)
                            voff = (kv*22 + ci)*128
                            first = oi == 0
                            sg = segs(qlo, qhi)
                            for si_, (s0q, s1q) in enumerate(sg):
                                T_.matmul(pctx[:, s0q:s1q], V_att[:, voff:voff+128],
                                          Pb[pbi][:, s0q:s1q], start=first, stop=False)
                                mmd = T_.matmul(pden[0:1, s0q:s1q], ones_col[:], Pb[pbi][:, s0q:s1q],
                                                start=False, stop=False)
                            eT.inc(mmd)
                            done_tok[oi] = eT.tok()

                        for oi in range(n_ord):
                            emit_smm(oi)
                            if oi >= 1:
                                proc(oi - 1, smm_tok[oi])
                        eT.inc(T_.matmul(pden[0:1, 0:512], ones_col[:], az[:, 0:512],
                                         start=False, stop=False))
                        proc(n_ord - 1, eT.tok())

                        for s0q in (0, 512):
                            T_.matmul(pctx[:, s0q:s0q+512], ident[:], az[:, 0:512], start=False,
                                      stop=True)
                            mmz = T_.matmul(pden[0:1, s0q:s0q+512], ones_col[:], az[:, 0:512],
                                            start=False, stop=True)
                        eT.inc(mmz)
                        acc_tok = eT.tok()
                        wait(V_, acc_tok)
                        V_.reciprocal(rb_row[0:1, 0:1024], pden[0:1, 0:1024])
                        V_.drain()
                        rec_tok = eV.tok()
                        wait(T_, rec_tok)
                        T_.matmul(sA[:, 0:512], ones_row[:], rb_row[0:1, 0:512], start=True, stop=True)
                        T_.matmul(sA[:, 512:1024], ones_row[:], rb_row[0:1, 512:1024], start=True, stop=True)
                        eT.inc(T_.matmul(sBk[:, 0:512], ident[:], az[:, 0:512], start=True, stop=True))
                        dn_open_war = None
                        wait(S_, eT.tok())
                        S_.copy(rb_sb[:, 0:1024], sA[:, 0:1024])
                        S_.drain()
                        prev_iter_s[0] = eS.tok()
                        wait(V_, prev_iter_s[0])
                        V_.tensor_tensor(qbuf[:, qhg*1024:(qhg+1)*1024], pctx[:, 0:1024],
                                         rb_sb[:, 0:1024], ALU.mult)
                        V_.drain()
                        prev_iter_v[0] = eV.tok()
                        dn_open_war = prev_iter_v[0]
            barrier()
            if stage == "attn":
                for j in range(NQH):
                    qS.dma(dbgctx[j*128:(j+1)*128, 0:1024], qbuf[:, j*1024:(j+1)*1024])
                qS.wait_all(SY)
                return

            with ExitStack() as p4:
                sb4 = lambda n, s, d: p4.enter_context(nc.sbuf_tensor(n, s, d))
                pD = [p4.enter_context(nc.psum_tensor(f"pD{i}", [128, 512], F32)) for i in range(2)]
                wdn = [sb4(f"wdn{i}", [128, NCH * 128], BF16) for i in range(2)]
                x1r = [sb4(f"x1r{i}", [128, 512], F32) for i in range(2)]
                x2t = [sb4(f"x2t{i}", [128, 512], F32) for i in range(2)]
                azd = sb4("azd", [128, 512], BF16)
                V_.memset(azd[:], 0.0)
                zd_tok = eV.tok()
                dn_war = [None, None]
                st_war = [None, None]
                wv_war = [None, None]
                for ot in range(NCH):
                    pa = ot % 2
                    if wv_war[pa] is not None:
                        wait(qG.eng, wv_war[pa])
                    w_tok = qG.dma(wdn[pa][:].rearrange("p (c n) -> p c n", n=128),
                                   WdnT[:, ot*128:(ot+1)*128].rearrange("(c p) n -> p c n", p=128))
                    wait(T_, w_tok)
                    if ot == 0:
                        wait(T_, zd_tok)
                    for g in range(2):
                        goff = g * 512
                        if dn_war[pa] is not None:
                            wait(SY, dn_war[pa])
                        xr_tok = qS.dma(x1r[pa][:, 0:512], x1T[ot*128:(ot+1)*128, goff:goff+512])
                        if dn_war[pa] is not None:
                            wait(T_, dn_war[pa])
                        for k in range(NCH):
                            T_.matmul(pD[pa][:, 0:512], wdn[pa][:, k*128:(k+1)*128],
                                      qbuf[:, k*1024 + goff: k*1024 + goff + 512],
                                      start=(k == 0), stop=False)
                        eT.inc(T_.matmul(pD[pa][:, 0:512], ident[:], azd[:, 0:512], start=False, stop=True))
                        wait(V_, eT.tok()); wait(V_, xr_tok)
                        if st_war[pa] is not None:
                            wait(V_, st_war[pa])
                        V_.scalar_tensor_tensor(x2t[pa][:, 0:512], pD[pa][:, 0:512], bdn[:, ot:ot+1],
                                                x1r[pa][:, 0:512], ALU.add, ALU.add)
                        V_.drain()
                        dn_war[pa] = eV.tok()
                        wait(SY, eV.tok())
                        st_war[pa] = qS.dma(x2T[ot*128:(ot+1)*128, goff:goff+512], x2t[pa][:, 0:512])
                    wv_war[pa] = eT.tok()
                qS.wait_all(SY)
        return

    emit_ffn(xT, x1T, FTOT, GROUPS, first_pass=True)
    barrier()

    if stage == "ffnA":
        ctx.close()
        return nc

    emit_attn(stage)
    barrier()
    if stage in ("qkv", "attn", "dense"):
        ctx.close()
        return nc

    emit_ffn(x2T, outT, OWN, [512, 512], first_pass=False, tag="fb_")
    barrier()
    ctx.close()
    return nc


def make_in_maps(inp):
    x = np.asarray(inp["L0_attn_hidden_states"])[0]
    w_qkv = np.asarray(inp["L0_attn_w_qkv"])
    b_qkv = np.asarray(inp["L0_attn_b_qkv"])
    w_dn = np.asarray(inp["L0_attn_w_dense"])
    b_dn = np.asarray(inp["L0_attn_b_dense"])
    Wg = np.asarray(inp["L0_ffn_Wg"]); Wu = np.asarray(inp["L0_ffn_Wu"])
    Wd = np.asarray(inp["L0_ffn_Wd"])
    ag = np.asarray(inp["L0_ffn_ag"]); au = np.asarray(inp["L0_ffn_au"])
    ad = np.asarray(inp["L0_ffn_ad"])

    C = np.ascontiguousarray
    WgT = C(Wg.T); WuT = C(Wu.T); WdT = C(Wd.T)
    qrows, bq, kvrows, bkv = [], [], [], []
    for kv in range(NKV):
        for j in range(4):
            qrows.append(w_qkv[kv*768 + j*128: kv*768 + (j+1)*128])
            bq.append(b_qkv[kv*768 + j*128: kv*768 + (j+1)*128])
        kvrows.append(w_qkv[kv*768 + 512: kv*768 + 768])
        bkv.append(b_qkv[kv*768 + 512: kv*768 + 768])
    WqT = C(np.concatenate(qrows, 0).T)
    WkvT = C(np.concatenate(kvrows, 0).T)
    bq = np.concatenate(bq)[None, :]
    bkv = np.concatenate(bkv)[None, :]
    WdnT = C(w_dn.T)

    shared = dict(
        WgT=WgT, WuT=WuT, WdT=WdT, WqT=WqT, WkvT=WkvT, WdnT=WdnT,
        agt_h=C(ag.reshape(NI, 128).T), aut_h=C(au.reshape(NI, 128).T),
        ad_h=C(ad.reshape(NCH, 128).T), bdn_h=C(b_dn.reshape(NCH, 128).T),
        bq_h=C(bq), bkv_h=C(bkv), tri_h=np.asarray(host_tri()),
    )
    nonce = np.full((1, 64), np.random.randint(1, 2**31 - 1), np.int32)
    in_maps = []
    for c in range(8):
        own, halo = host_core_tokens(c)
        allt = np.concatenate([own, halo, np.full(FTOT - TOT, -1)])
        xT = C(gather_tokens(x, allt).T)
        cq, sq = host_rope_tables(own, SM_SCALE)
        kt = np.concatenate([own, halo, np.full(1920 - TOT, -1)])
        ck, sk = host_rope_tables(kt, 1.0)
        m = dict(shared)
        m.update(xT=xT, cosq_h=cq, sinmq_h=sq, cosk_h=ck, sinmk_h=sk,
                 fmask_h=np.asarray(host_factor_masks(c)), nonce_h=nonce)
        in_maps.append(m)
    return in_maps

def assemble_output(results):
    out = np.zeros((SEQ, D), np.float32)
    for c in range(8):
        own, _ = host_core_tokens(c)
        out[own] = np.asarray(results[c]["outT"]).T
    return out[None]

def gather_tokens(x_flat, idx):
    out = np.zeros((len(idx), x_flat.shape[1]), x_flat.dtype)
    ok = idx >= 0
    out[ok] = x_flat[idx[ok]]
    return out


from concourse.bass_utils import run_bass_kernel_spmd
import concourse.bass_utils as _bu

_orig_pnp = _bu._process_ntff_profile
def _pnp_lowmem(*a, **k):
    import gc
    try:
        import jax
        for arr in list(jax.live_arrays()):
            try:
                arr.delete()
            except Exception:
                pass
        jax.clear_caches()
    except Exception:
        pass
    gc.collect()
    try:
        import ctypes
        ctypes.CDLL("libc.so.6").malloc_trim(0)
    except Exception:
        pass
    return _orig_pnp(*a, **k)
_bu._process_ntff_profile = _pnp_lowmem

_NC_CACHE = {}

def kernel(**inputs):
    in_maps = make_in_maps(inputs)
    nc = build_program(stage="full")
    res = run_bass_kernel_spmd(nc, in_maps, core_ids=list(range(8)))
    out = np.zeros((SEQ, D), np.float32)
    for c in range(8):
        own, _ = host_core_tokens(c)
        out[own] = np.asarray(res.results[c]["outT"]).T
    return out[None]
